# version 10
# baseline (speedup 1.0000x reference)
import contextlib
import os
import numpy as np
import ml_dtypes
import concourse.bass as bass
import concourse.mybir as mybir
from concourse.bass_utils import run_bass_kernel_spmd

F32 = mybir.dt.float32
BF16 = mybir.dt.bfloat16
AX = mybir.AxisListType.X
OP = mybir.AluOpType
AF = mybir.ActivationFunctionType

D = 2048
NTOK = 4096
NT = 32
MAGIC = float(np.float32(1.5 * 2 ** 23))
EPS = 1e-5
INV_D = 1.0 / 2048.0
TINY = 1e-30

ENGS = ("sync", "gpsimd", "tensor", "vector", "scalar")


class Prog:
    def __init__(self):
        self.ops = {e: [] for e in ENGS}
        self.cnt = {}
        self.last_wait = {}
        self.keys = {}

    def sem(self, name):
        self.cnt[name] = 0

    def wait(self, eng, sem, val):
        if val is None or val <= 0:
            return
        key = (eng, sem)
        if self.last_wait.get(key, -1) >= val:
            return
        self.last_wait[key] = val
        self.ops[eng].append(("w", sem, val))

    def op(self, eng, fn, inc=None, n=1, nodrain=False):
        drain = eng in ("scalar", "vector") and not nodrain
        if inc is not None:
            self.cnt[inc] += n
            if drain:
                self.ops[eng].append(("o", fn))
                self.ops[eng].append(("i", lambda e: e.drain(), inc, n))
            else:
                self.ops[eng].append(("i", fn, inc, n))
        else:
            self.ops[eng].append(("o", fn))
            if drain:
                self.ops[eng].append(("o", lambda e: e.drain()))

    def sop(self, fn, inc=None, n=1):
        self.op("scalar", fn, inc=inc, n=n, nodrain=False)

    def c(self, sem):
        return self.cnt[sem]

    def key(self, name, val=None):
        if val is None:
            return self.keys.get(name, 0)
        self.keys[name] = val

    def replay(self, eng_name, eng, sems):
        for item in self.ops[eng_name]:
            if item[0] == "w":
                eng.wait_ge(sems[item[1]], item[2])
            elif item[0] == "i":
                item[1](eng).then_inc(sems[item[2]], item[3])
            else:
                item[1](eng)


def build_nc(stage="full"):
    nc = bass.Bass("TRN2", target_bir_lowering=False)

    x_h = nc.dram_tensor("x", [NTOK, D], F32, kind="ExternalInput").ap()
    wi_h = nc.dram_tensor("wi", [D, D], F32, kind="ExternalInput").ap()
    wd_h = nc.dram_tensor("wd", [D, D], F32, kind="ExternalInput").ap()
    wg_h = nc.dram_tensor("wg", [D, D], F32, kind="ExternalInput").ap()
    wo_h = nc.dram_tensor("wo", [D, D], F32, kind="ExternalInput").ap()
    wk_h = nc.dram_tensor("wk", [D, D], F32, kind="ExternalInput").ap()
    bin_h = nc.dram_tensor("bin", [D], F32, kind="ExternalInput").ap()
    bdec_h = nc.dram_tensor("bdec", [D], F32, kind="ExternalInput").ap()
    bgate_h = nc.dram_tensor("bgate", [D], F32, kind="ExternalInput").ap()
    bout_h = nc.dram_tensor("bout", [1, D], F32, kind="ExternalInput").ap()
    dbias_h = nc.dram_tensor("dbias", [D], F32, kind="ExternalInput").ap()
    fbias_h = nc.dram_tensor("fbias", [1, D], F32, kind="ExternalInput").ap()
    eyef_h = nc.dram_tensor("eyef", [128, 128], F32, kind="ExternalInput").ap()
    ones_h = nc.dram_tensor("onesb", [1, 128], BF16, kind="ExternalInput").ap()

    out_h = nc.dram_tensor("out", [NTOK, D], F32, kind="ExternalOutput").ap()
    x1_h = nc.dram_tensor("x1h", [NTOK, D], F32, kind="Internal").ap()
    y_h = nc.dram_tensor("yh", [16, 128, NTOK], BF16, kind="Internal").ap()
    kq_h = nc.dram_tensor("kqh", [128, 32768], BF16, kind="Internal").ap()

    p = Prog()
    for s in ("LD", "LW", "SX1", "SY", "SOUT", "SKQ", "TP", "VP", "SP",
              "TQ", "SR", "SEV", "VS", "TX"):
        p.sem(s)

    with contextlib.ExitStack() as es:
        ec = es.enter_context
        WB = ec(nc.sbuf_tensor("WB", [128, 75776], BF16))
        FA = ec(nc.sbuf_tensor("FA", [128, 12288], F32))
        statn = ["sums", "sumsq", "sumsq2", "m", "mm", "var", "vpe", "lnv",
                 "rstd", "nrstd", "nmr", "amax", "amcl", "rcp", "inv", "sx",
                 "amax2", "am2", "rcp2", "inv2", "alpha"]
        st2 = [
            {n: ec(nc.sbuf_tensor(f"st{q}_{n}", [128, 1], F32)) for n in statn}
            for q in range(2)
        ]
        stb = {n: ec(nc.sbuf_tensor("stb_" + n, [128, 1], F32))
               for n in ["sums", "sumsq", "sumsq2", "m", "mm", "var", "vpe",
                         "lnv", "nrstd"]}
        rstd_tab = ec(nc.sbuf_tensor("rstd_tab", [128, 32], F32))
        nmr_tab = ec(nc.sbuf_tensor("nmr_tab", [128, 32], F32))
        vpe_tab = ec(nc.sbuf_tensor("vpe_tab", [128, 32], F32))
        m_tab = ec(nc.sbuf_tensor("m_tab", [128, 32], F32))
        lnv_tab = ec(nc.sbuf_tensor("lnv_tab", [128, 32], F32))
        bin_c = ec(nc.sbuf_tensor("bin_c", [128, 16], F32))
        bgate_c = ec(nc.sbuf_tensor("bgate_c", [128, 16], F32))
        bdec_c = ec(nc.sbuf_tensor("bdec_c", [128, 16], F32))
        dbias_c = ec(nc.sbuf_tensor("dbias_c", [128, 16], F32))
        bd_c = ec(nc.sbuf_tensor("bd_c", [128, 16], F32))
        nbd_c = ec(nc.sbuf_tensor("nbd_c", [128, 16], F32))
        fbamax = ec(nc.sbuf_tensor("fbamax", [1, 1], F32))
        fbcl = ec(nc.sbuf_tensor("fbcl", [1, 1], F32))
        fbrcp = ec(nc.sbuf_tensor("fbrcp", [1, 1], F32))
        fbinv = ec(nc.sbuf_tensor("fbinv", [1, 1], F32))
        fbsk = ec(nc.sbuf_tensor("fbsk", [1, 1], F32))
        bq_bf = ec(nc.sbuf_tensor("bq_bf", [1, D], BF16))
        bout_bf = ec(nc.sbuf_tensor("bout_bf", [1, D], BF16))
        carry = ec(nc.sbuf_tensor("carry", [128, 32], F32))
        cmag = ec(nc.sbuf_tensor("cmag", [128, 1], F32))
        cneg = ec(nc.sbuf_tensor("cneg", [128, 1], F32))
        cnh = ec(nc.sbuf_tensor("cnh", [128, 1], F32))
        eyef = ec(nc.sbuf_tensor("eyef_s", [128, 128], F32))
        onesb = ec(nc.sbuf_tensor("onesb_s", [1, 128], BF16))
        pz = ec(nc.psum_tensor("pz", [128, 4096], F32))
        sem_handles = {nm: ec(nc.semaphore(nm)) for nm in p.cnt.keys()}

        def bank(i):
            return pz[:, i * 512:(i + 1) * 512]

        def bank2(i):
            return pz[:, i * 512:(i + 2) * 512]

        def wb(a, b):
            return WB[:, a:b]

        def fa(a, b):
            return FA[:, a:b]

        xb = [fa(0, 2048), fa(2048, 4096)]
        dumpV = fa(4096, 6144)
        dumpS = [fa(6144, 8192), fa(8192, 10240)]
        ybuf = fa(10240, 12288)
        d_stg = [fa(6144, 7168), fa(7168, 8192)]
        b_stg = [fa(8192, 9216), fa(9216, 10240)]
        s_stg = [fa(10240, 11264), fa(11264, 12288)]

        kqA = wb(0, 32768)
        yTb = [wb(65536, 67584), wb(67584, 69632)]
        qb_s = wb(69632, 71680)
        qbuf = [qb_s, qb_s]
        qT = [wb(71680, 73728), wb(73728, 75776)]
        yb16 = [qb_s, qb_s]
        kqTst = wb(45056, 47104)
        ln2T = [wb(0, 16384), wb(16384, 32768)]
        wslot = [wb(32768 + i * 2048, 32768 + (i + 1) * 2048) for i in range(6)]
        ln2b = [wb(45056, 47104), wb(47104, 49152)]
        tsig_st = wb(49152, 50176)
        z_st = wb(50176, 51200)
        inp_st = [wb(51200, 52224), wb(52224, 53248)]
        e_stg = [wb(53248, 54272), wb(54272, 55296)]
        gate_st = [wb(55296, 56320), wb(56320, 57344)]
        y_stg = [wb(57344, 58368), wb(58368, 59392)]
        woB = wb(0, 32768)
        kqB = wb(32768, 65536)

        def dma(eng, out, in_, inc, slow=False):
            if slow:
                p.op(eng, lambda e, o=out, i=in_: e.dma_start(
                    out=o, in_=i, allow_slow_non_contiguous=True), inc=inc, n=16)
            else:
                p.op(eng, lambda e, o=out, i=in_: e.dma_start(out=o, in_=i),
                     inc=inc, n=16)

        def tdma(out, in_):
            p.op("sync", lambda e, o=out, i=in_: e.dma_start(out=o, in_=i,
                                                             transpose=True),
                 inc="TX", n=16)

        p.op("vector", lambda e: e.memset(cmag[:], MAGIC))
        p.op("vector", lambda e: e.memset(cneg[:], -1.0))
        p.op("vector", lambda e: e.memset(cnh[:], -0.5))
        nc.const_aps.register(F32, MAGIC, cmag[:])
        nc.const_aps.register(F32, -1.0, cneg[:])
        nc.const_aps.register(F32, -0.5, cnh[:])
        dma("sync", eyef[:], eyef_h, "LD")
        dma("sync", onesb[:], ones_h, "LD")
        fbrow = FA[0:1, 4096:6144]
        fbt = FA[0:1, 6144:8192]
        dma("sync", fbrow, fbias_h, "LD")
        dma("gpsimd", bin_c[:], bin_h.rearrange("(c p) -> p c", p=128), "LW", slow=True)
        dma("gpsimd", bgate_c[:], bgate_h.rearrange("(c p) -> p c", p=128), "LW", slow=True)
        dma("gpsimd", bdec_c[:], bdec_h.rearrange("(c p) -> p c", p=128), "LW", slow=True)
        dma("gpsimd", dbias_c[:], dbias_h.rearrange("(c p) -> p c", p=128), "LW", slow=True)
        dma("gpsimd", bout_bf[:], bout_h, "LW")
        ld_consts = p.c("LD")
        lw_consts = p.c("LW")

        p.wait("vector", "LW", lw_consts)
        p.op("vector", lambda e: e.tensor_add(bd_c[:], bdec_c[:], dbias_c[:]))
        p.op("vector", lambda e: e.tensor_scalar(nbd_c[:], bd_c[:], -1.0, None,
                                                 op0=OP.mult))
        p.op("vector", lambda e: e.memset(carry[:], 0.0))
        p.wait("vector", "LD", ld_consts)
        p.op("vector", lambda e: e.reduce_max(fbamax[:], fbrow, axis=AX,
                                              apply_absolute_value=True))
        p.op("vector", lambda e: e.tensor_scalar(fbcl[:], fbamax[:], TINY, None,
                                                 op0=OP.max))
        p.op("vector", lambda e: e.reciprocal(fbrcp[:], fbcl[:]))
        p.op("vector", lambda e: e.tensor_scalar(fbinv[:], fbrcp[:], 127.0, None,
                                                 op0=OP.mult))
        p.op("vector", lambda e: e.tensor_scalar(fbsk[:], fbcl[:], 1.0 / 127.0,
                                                 None, op0=OP.mult))
        p.op("vector", lambda e: e.tensor_scalar(fbt, fbrow, fbinv[0:1, :], MAGIC,
                                                 op0=OP.mult, op1=OP.add))
        p.op("vector", lambda e: e.tensor_scalar(bq_bf[:], fbt, -MAGIC,
                                                 fbsk[0:1, :],
                                                 op0=OP.add, op1=OP.mult), inc="VP")

        st = st2[0]
        wkb = [qT[0], qT[1]]
        wkT = qbuf[0]
        for fc in range(16):
            buf = wkb[fc % 2]
            p.wait("gpsimd", "TX", p.key("kqld%d" % (fc - 2)))
            src_ap = wk_h[:, fc * 128:(fc + 1) * 128].rearrange("(k p) f -> p k f", p=128)
            dma("gpsimd", buf.rearrange("p (k f) -> p k f", k=16), src_ap, "LW")
            ld_snap = p.c("LW")
            p.wait("sync", "LW", ld_snap)
            p.wait("sync", "VP", p.key("kqq%d" % (fc - 1)))
            for kc in range(16):
                tdma(wkT[:, kc * 128:(kc + 1) * 128],
                     buf[:, kc * 128:(kc + 1) * 128])
            p.key("kqld%d" % fc, p.c("TX"))
            p.wait("vector", "TX", p.c("TX"))
            p.op("vector", lambda e: e.reduce_max(st["amax"][:], wkT[:], axis=AX,
                                                  apply_absolute_value=True))
            p.op("vector", lambda e: e.tensor_scalar(st["amcl"][:], st["amax"][:],
                                                     TINY, None, op0=OP.max))
            p.op("vector", lambda e: e.reciprocal(st["rcp"][:], st["amcl"][:]))
            p.op("vector", lambda e: e.tensor_scalar(st["inv"][:], st["rcp"][:],
                                                     127.0, None, op0=OP.mult))
            p.op("vector", lambda e: e.tensor_scalar(st["sx"][:], st["amcl"][:],
                                                     1.0 / 127.0, None, op0=OP.mult))
            p.op("vector", lambda e: e.tensor_scalar(dumpS[0], wkT[:], st["inv"][:],
                                                     MAGIC, op0=OP.mult, op1=OP.add))
            p.op("vector", lambda e: e.tensor_scalar(kqTst[:], dumpS[0], -MAGIC,
                                                     st["sx"][:],
                                                     op0=OP.add, op1=OP.mult), inc="VP")
            p.key("kqq%d" % fc, p.c("VP"))
            p.wait("sync", "VP", p.c("VP"))
            for kc in range(16):
                tdma(kqA[:, kc * 2048 + fc * 128: kc * 2048 + (fc + 1) * 128],
                     kqTst[:, kc * 128:(kc + 1) * 128])
            p.wait("vector", "TX", p.c("TX"))
        tx_kq_done = p.c("TX")
        prep_vp = p.c("VP")
        p.wait("gpsimd", "TX", tx_kq_done)
        dma("gpsimd", kq_h, kqA, "SKQ")
        skq_store = p.c("SKQ")

        fst = {"use": 0, "info": {}}

        def ffn_front(xsrc, kq_region, tab_j=None):
            u = fst["use"]
            fst["use"] = u + 1
            pr = u % 2
            st = st2[pr]
            fst["info"][u] = (xsrc, kq_region, tab_j)
            p.op("vector", lambda e: e.reduce_sum(st["sums"][:], xsrc, axis=AX))
            p.op("scalar", lambda e: e.activation(
                bank2(6), xsrc[:, 0:1024], AF.Square,
                accum_out=st["sumsq"][:]))
            p.op("scalar", lambda e: e.activation(
                bank2(6), xsrc[:, 1024:2048], AF.Square,
                accum_out=st["sumsq2"][:]), inc="SP")
            p.key("ffnsq%d" % u, p.c("SP"))
            p.wait("vector", "SP", p.c("SP"))
            p.op("vector", lambda e: e.tensor_add(st["sumsq"][:], st["sumsq"][:],
                                                  st["sumsq2"][:]))
            p.op("vector", lambda e: e.tensor_scalar(st["m"][:], st["sums"][:],
                                                     INV_D, None, op0=OP.mult))
            p.op("vector", lambda e: e.tensor_mul(st["mm"][:], st["m"][:], st["m"][:]))
            p.op("vector", lambda e: e.scalar_tensor_tensor(
                st["var"][:], st["sumsq"][:], INV_D, st["mm"][:],
                op0=OP.mult, op1=OP.subtract))
            p.op("vector", lambda e: e.tensor_scalar(st["vpe"][:], st["var"][:],
                                                     EPS, None, op0=OP.add), inc="VP")
            p.wait("scalar", "VP", p.c("VP"))
            p.sop(lambda e: e.activation(st["lnv"][:], st["vpe"][:], AF.Ln))
            p.sop(lambda e: e.activation(st["rstd"][:], st["lnv"][:],
                                         AF.Exp, scale=-0.5))
            p.sop(lambda e: e.activation(st["nrstd"][:], st["rstd"][:],
                                         AF.Identity, scale=-1.0))
            p.sop(lambda e: e.activation(st["nmr"][:], st["m"][:],
                                         AF.Identity, scale=st["nrstd"][:]))
            p.op("scalar", lambda e, pp=pr: e.activation(dumpS[pp], xsrc, AF.Identity,
                                                         bias=st["nmr"][:],
                                                         scale=st["rstd"][:]), inc="SP")
            p.wait("vector", "SP", p.c("SP"))
            p.op("vector", lambda e, pp=pr: e.reduce_max(st["amax"][:], dumpS[pp],
                                                         axis=AX,
                                                         apply_absolute_value=True))
            p.op("vector", lambda e: e.tensor_scalar(st["amcl"][:], st["amax"][:],
                                                     TINY, None, op0=OP.max))
            p.op("vector", lambda e: e.reciprocal(st["rcp"][:], st["amcl"][:]))
            p.op("vector", lambda e: e.tensor_scalar(st["inv"][:], st["rcp"][:],
                                                     127.0, None, op0=OP.mult))
            p.op("vector", lambda e: e.tensor_scalar(st["sx"][:], st["amcl"][:],
                                                     1.0 / 127.0, None, op0=OP.mult))
            p.op("vector", lambda e, pp=pr: e.tensor_scalar(
                dumpS[1 - pp], dumpS[pp], st["inv"][:], MAGIC,
                op0=OP.mult, op1=OP.add))
            p.wait("vector", "TX", p.key("tdma%d" % (u - 1)))
            p.op("vector", lambda e, pp=pr, q=qbuf[pr]: e.tensor_scalar(
                q, dumpS[1 - pp], -MAGIC, st["sx"][:], op0=OP.add, op1=OP.mult),
                inc="VP")
            vp_xq = p.c("VP")
            p.key("ffnxq%d" % u, vp_xq)
            p.wait("sync", "VP", vp_xq)
            p.wait("sync", "TQ", p.key("mmq%d" % (u - 2)))
            p.wait("sync", "VP", p.key("q2f%d" % (u - 2)))
            for kc in range(16):
                tdma(qT[pr][:, kc * 128:(kc + 1) * 128],
                     qbuf[pr][:, kc * 128:(kc + 1) * 128])
            p.key("tdma%d" % u, p.c("TX"))

        def ffn_back(u, yb_gate):
            pr = u % 2
            st = st2[pr]
            xsrc, kq_region, tab_j = fst["info"].pop(u)
            p.wait("tensor", "TX", p.key("tdma%d" % u))
            for q4 in range(4):
                bk = bank(q4)
                p.wait("tensor", "SR", p.key("sr%d_%d" % (u - 1, q4 // 2)))
                for kc in range(16):
                    p.op("tensor", lambda e, b=bk, k=kc, jp=pr, q=q4, kr=kq_region:
                         e.matmul(b[:], qT[jp][:, k * 128:(k + 1) * 128],
                                  kr[:, k * 2048 + q * 512: k * 2048 + (q + 1) * 512],
                                  start=(k == 0), stop=False))
                p.op("tensor", lambda e, b=bk, q=q4: e.matmul(
                    b[:], onesb[:], bq_bf[0:1, q * 512:(q + 1) * 512],
                    start=False, stop=True), inc="TQ")
                if q4 % 2 == 1:
                    tq_snap = p.c("TQ")
                    p.wait("scalar", "TQ", tq_snap)
                    if yb_gate is not None:
                        p.wait("scalar", yb_gate[0], yb_gate[1])
                    h = q4 // 2
                    p.op("scalar", lambda e, hh=h: e.activation(
                        ybuf[:, hh * 1024:(hh + 1) * 1024], bank2(hh * 2), AF.Relu),
                        inc="SR")
                    p.key("sr%d_%d" % (u, h), p.c("SR"))
            p.key("mmq%d" % u, p.c("TQ"))
            p.wait("vector", "SR", p.c("SR"))
            p.op("vector", lambda e: e.reduce_max(st["amax2"][:], ybuf[:], axis=AX))
            p.op("vector", lambda e: e.tensor_scalar(st["am2"][:], st["amax2"][:],
                                                     TINY, None, op0=OP.max))
            p.op("vector", lambda e: e.reciprocal(st["rcp2"][:], st["am2"][:]))
            p.op("vector", lambda e: e.tensor_scalar(st["inv2"][:], st["rcp2"][:],
                                                     127.0, None, op0=OP.mult))
            p.op("vector", lambda e: e.tensor_scalar(st["alpha"][:], st["am2"][:],
                                                     0.5 / 127.0, None, op0=OP.mult))
            p.op("vector", lambda e, pp=pr: e.tensor_scalar(
                dumpS[pp], ybuf[:], st["inv2"][:], MAGIC, op0=OP.mult, op1=OP.add))
            p.wait("vector", "TQ", p.key("mmq%d" % u))
            p.op("vector", lambda e, q=qT[pr], pp=pr: e.tensor_scalar(
                q, dumpS[pp], -MAGIC, None, op0=OP.add))
            if yb_gate is not None:
                p.wait("vector", yb_gate[0], yb_gate[1])
            p.wait("vector", "SP", p.key("tabsq%d" % (u - 1)))
            p.op("vector", lambda e, q=qT[pr]: e.scalar_tensor_tensor(
                ybuf[:], q, st["alpha"][:], xsrc, op0=OP.mult, op1=OP.add),
                inc="VP")
            res_vp = p.c("VP")
            p.key("q2f%d" % u, res_vp)
            if tab_j is not None:
                p.op("vector", lambda e: e.reduce_sum(stb["sums"][:], ybuf[:],
                                                      axis=AX))
                p.wait("scalar", "VP", res_vp)
                p.op("scalar", lambda e: e.activation(
                    bank2(6), ybuf[:, 0:1024], AF.Square,
                    accum_out=stb["sumsq"][:]))
                p.op("scalar", lambda e: e.activation(
                    bank2(6), ybuf[:, 1024:2048], AF.Square,
                    accum_out=stb["sumsq2"][:]), inc="SP")
                p.key("tabsq%d" % u, p.c("SP"))
                p.wait("vector", "SP", p.c("SP"))
                p.op("vector", lambda e: e.tensor_add(stb["sumsq"][:],
                                                      stb["sumsq"][:],
                                                      stb["sumsq2"][:]))
                p.op("vector", lambda e, t=tab_j: e.tensor_scalar(
                    m_tab[:, t:t + 1], stb["sums"][:], INV_D, None, op0=OP.mult))
                p.op("vector", lambda e, t=tab_j: e.tensor_mul(
                    stb["mm"][:], m_tab[:, t:t + 1], m_tab[:, t:t + 1]))
                p.op("vector", lambda e, t=tab_j: e.scalar_tensor_tensor(
                    stb["var"][:], stb["sumsq"][:], INV_D, stb["mm"][:],
                    op0=OP.mult, op1=OP.subtract))
                p.op("vector", lambda e, t=tab_j: e.tensor_scalar(
                    vpe_tab[:, t:t + 1], stb["var"][:], EPS, None, op0=OP.add),
                    inc="VP")
            return res_vp

        p.wait("tensor", "VP", prep_vp)
        xb3 = [xb[0], xb[1], dumpV]
        for j in range(NT):
            if j < 3:
                p.wait("sync", "TX", tx_kq_done)
            p.wait("sync", "SX1", 16 * (j - 2))
            dma("sync", xb3[j % 3], x_h[j * 128:(j + 1) * 128, :], "LD")
            ld_snap = p.c("LD")
            p.wait("vector", "LD", ld_snap)
            p.wait("scalar", "LD", ld_snap)
            ffn_front(xb3[j % 3], kqA, tab_j=j)
            if j >= 1:
                vpr = ffn_back(fst["use"] - 2, ("SX1", 16 * (j - 1)))
                p.wait("sync", "VP", vpr)
                dma("sync", x1_h[(j - 1) * 128:j * 128, :], ybuf[:], "SX1")
        vpr = ffn_back(fst["use"] - 1, ("SX1", 16 * (NT - 1)))
        p.wait("sync", "VP", vpr)
        dma("sync", x1_h[(NT - 1) * 128:NT * 128, :], ybuf[:], "SX1")
        sx1_all = p.c("SX1")
        p.wait("scalar", "VP", p.c("VP"))
        p.sop(lambda e: e.activation(lnv_tab[:], vpe_tab[:], AF.Ln))
        p.op("scalar", lambda e: e.activation(rstd_tab[:], lnv_tab[:], AF.Exp,
                                              scale=-0.5), inc="SP")
        p.wait("vector", "SP", p.c("SP"))
        p.op("vector", lambda e: e.tensor_mul(lnv_tab[:], m_tab[:], rstd_tab[:]))
        p.op("vector", lambda e: e.tensor_scalar(nmr_tab[:], lnv_tab[:], -1.0,
                                                 None, op0=OP.mult), inc="VP")
        tabs_vp = p.c("VP")

        if stage == "A":
            o_tab = nc.dram_tensor("tabdump", [128, 64], F32,
                                   kind="ExternalOutput").ap()
            p.wait("sync", "SX1", sx1_all)
            p.wait("sync", "SP", p.c("SP"))
            dma("sync", out_h, x1_h, "SOUT")
            dma("sync", o_tab[:, 0:32], rstd_tab[:], "SOUT")
            dma("sync", o_tab[:, 32:64], nmr_tab[:], "SOUT")
            p.wait("sync", "SOUT", p.c("SOUT"))
            p.wait("gpsimd", "SKQ", skq_store)
            emit_into_block(nc, p, sem_handles)
            return nc

        wsrcs = [wi_h, wg_h, wd_h]

        def emit_wload(c):
            base = (c % 2) * 3
            for wi_, wsrc in enumerate(wsrcs):
                src = wsrc[:, c * 128:(c + 1) * 128].rearrange("(k p) f -> p k f", p=128)
                dst = wslot[base + wi_].rearrange("p (k f) -> p k f", k=16)
                dma("gpsimd", dst, src, "LW")
            p.key("wl%d" % (c % 2), p.c("LW"))

        emit_wload(0)
        emit_wload(1)

        p.wait("tensor", "SR", p.c("SR"))
        b1 = {"pmuse": 0}
        lnsnaps = []

        for s in range(4):
            tiles = [4 * s + i for i in range(4)] + [16 + 4 * s + i for i in range(4)]
            for i, tj in enumerate(tiles):
                gi = s * 8 + i
                p.wait("sync", "SX1", sx1_all)
                if gi >= 2:
                    p.wait("sync", "SP", lnsnaps[gi - 2])
                dma("sync", xb[i % 2], x1_h[tj * 128:(tj + 1) * 128, :], "LD")
                ld_snap = p.c("LD")
                p.wait("scalar", "LD", ld_snap)
                p.wait("scalar", "VP", tabs_vp)
                p.wait("scalar", "TX", p.key("b1tt%d" % (gi - 2)))
                p.op("scalar", lambda e, b=xb[i % 2], o=ln2b[i % 2], t=tj:
                     e.activation(o[:], b, AF.Identity,
                                  bias=nmr_tab[:, t:t + 1],
                                  scale=rstd_tab[:, t:t + 1]), inc="SP")
                lnsnaps.append(p.c("SP"))
                p.wait("sync", "SP", p.c("SP"))
                p.wait("sync", "TQ", p.key("b1mm%d" % (s - 2)))
                for kc in range(16):
                    tdma(ln2T[s % 2][:, kc * 1024 + i * 128: kc * 1024 + (i + 1) * 128],
                         ln2b[i % 2][:, kc * 128:(kc + 1) * 128])
                p.key("b1tt%d" % gi, p.c("TX"))
            tx_span_ready = p.c("TX")

            for c in range(16):
                gc = s * 16 + c
                cs = c % 2
                p.wait("tensor", "LW", p.key("wl%d" % (c % 2)))
                p.wait("tensor", "TX", tx_span_ready)
                banks = {}
                for wi_ in range(3):
                    for h in range(2):
                        u = b1["pmuse"]
                        bk = bank(u % 8)
                        p.wait("tensor", "SEV", p.key("pev%d" % (u % 8)))
                        b1["pmuse"] = u + 1
                        banks[(wi_, h)] = (bk, u)
                        for kc in range(16):
                            p.op("tensor",
                                 lambda e, b=bk, k=kc, w=wslot[(c % 2) * 3 + wi_],
                                 sp_=s, hh=h:
                                 e.matmul(b[:], w[:, k * 128:(k + 1) * 128],
                                          ln2T[sp_ % 2][:, k * 1024 + hh * 512:
                                                        k * 1024 + (hh + 1) * 512],
                                          start=(k == 0), stop=(k == 15)),
                                 inc=("TQ" if kc == 15 else None))
                        p.key("b1u%d" % u, p.c("TQ"))
                p.key("b1mm%d" % s, p.c("TQ"))
                tq_chunk = p.c("TQ")
                nxt = c + 2
                if nxt < 16:
                    p.wait("gpsimd", "TQ", tq_chunk)
                    emit_wload(nxt)
                elif s < 3:
                    p.wait("gpsimd", "TQ", tq_chunk)
                    emit_wload(nxt - 16)
                p.wait("scalar", "VS", p.key("b1v%d" % (gc - 1)))
                for h in range(2):
                    bk, u = banks[(0, h)]
                    p.wait("scalar", "TQ", p.key("b1u%d" % u))
                    p.op("scalar", lambda e, bb=bk, hh=h, bc=bin_c[:, c:c + 1]:
                         e.activation(tsig_st[:, hh * 512:(hh + 1) * 512], bb[:],
                                      AF.Sigmoid, bias=bc), inc="SP")
                    p.op("scalar", lambda e, bb=bk, hh=h, bc=bin_c[:, c:c + 1]:
                         e.activation(z_st[:, hh * 512:(hh + 1) * 512], bb[:],
                                      AF.Identity, bias=bc), inc="SEV")
                    p.key("pev%d" % (u % 8), p.c("SEV"))
                for h in range(2):
                    bk, u = banks[(1, h)]
                    p.wait("scalar", "TQ", p.key("b1u%d" % u))
                    p.op("scalar", lambda e, bb=bk, hh=h, bc=bgate_c[:, c:c + 1],
                         g=gate_st[cs]:
                         e.activation(g[:, hh * 512:(hh + 1) * 512], bb[:],
                                      AF.Sigmoid, bias=bc), inc="SEV")
                    p.key("pev%d" % (u % 8), p.c("SEV"))
                for h in range(2):
                    bk, u = banks[(2, h)]
                    p.wait("scalar", "TQ", p.key("b1u%d" % u))
                    p.op("scalar", lambda e, bb=bk, hh=h, bc=nbd_c[:, c:c + 1],
                         ee=e_stg[cs]:
                         e.activation(ee[:, hh * 512:(hh + 1) * 512], bb[:],
                                      AF.Sigmoid, bias=bc, scale=cneg[:]), inc="SP")
                    p.op("scalar", lambda e, bb=bk, hh=h, bc=bd_c[:, c:c + 1],
                         dd=d_stg[cs]:
                         e.activation(dd[:, hh * 512:(hh + 1) * 512], bb[:],
                                      AF.Sigmoid, bias=bc), inc="SEV")
                    p.key("pev%d" % (u % 8), p.c("SEV"))
                sev_chunk = p.c("SEV")
                sp_chunk = p.c("SP")
                p.wait("vector", "SEV", sev_chunk)
                p.wait("vector", "SP", sp_chunk)
                p.op("vector", lambda e, ii=inp_st[cs]: e.tensor_mul(ii[:], tsig_st[:],
                                                                     z_st[:]))
                p.op("vector", lambda e, ii=inp_st[cs], ee=e_stg[cs], bb=b_stg[cs]:
                     e.tensor_mul(bb[:], ee[:], ii[:]))
                for h in range(2):
                    p.op("vector", lambda e, dd=d_stg[cs], bb=b_stg[cs], ss=s_stg[cs],
                         hh=h, cc=c:
                         e.tensor_tensor_scan(ss[:, hh * 512:(hh + 1) * 512],
                                              dd[:, hh * 512:(hh + 1) * 512],
                                              bb[:, hh * 512:(hh + 1) * 512],
                                              carry[:, cc * 2 + hh:cc * 2 + hh + 1],
                                              op0=OP.mult, op1=OP.add))
                for h in range(2):
                    p.op("vector", lambda e, ss=s_stg[cs], hh=h, cc=c: e.tensor_copy(
                        carry[:, cc * 2 + hh:cc * 2 + hh + 1],
                        ss[:, hh * 512 + 511:hh * 512 + 512]))
                p.wait("vector", "SY", p.key("b1y%d" % (gc - 2)))
                p.op("vector", lambda e, g=gate_st[cs], ss=s_stg[cs], yy=y_stg[cs]:
                     e.tensor_mul(yy[:], g[:], ss[:]), inc="VS")
                p.key("b1v%d" % gc, p.c("VS"))
                p.wait("sync", "VS", p.c("VS"))
                ydst = y_h[c, :, :].rearrange("p (b t) -> p b t", b=2)[:, :,
                                                                      s * 512:(s + 1) * 512]
                dma("sync", ydst, y_stg[cs][:].rearrange("p (b t) -> p b t", b=2),
                    "SY")
                p.key("b1y%d" % gc, p.c("SY"))
        sy_all = p.c("SY")

        if stage == "B1":
            o_yd = nc.dram_tensor("ydump", [16, 128, NTOK], BF16,
                                  kind="ExternalOutput").ap()
            p.wait("gpsimd", "SY", sy_all)
            for cc in range(16):
                p.op("gpsimd", lambda e, c=cc: e.dma_start(
                    out=o_yd[c], in_=y_h[c]), inc="SOUT", n=16)
            p.wait("gpsimd", "SOUT", p.c("SOUT"))
            p.wait("gpsimd", "SKQ", skq_store)
            emit_into_block(nc, p, sem_handles)
            return nc

        p.wait("gpsimd", "TQ", p.c("TQ"))
        p.wait("gpsimd", "VS", p.c("VS"))
        src = wo_h.rearrange("(k p) f -> p k f", p=128)
        dma("gpsimd", woB.rearrange("p (k f) -> p k f", k=16), src, "LW")
        p.wait("gpsimd", "SKQ", skq_store)
        dma("gpsimd", kqB, kq_h, "LW")
        lw_b2 = p.c("LW")

        p.wait("tensor", "SEV", p.c("SEV"))
        fst["use"] += 2
        base = fst["use"]
        p.key("mmq%d" % (base - 2), p.c("TQ"))
        p.key("tdma%d" % (base - 1), p.c("TX"))
        for h in range(2):
            p.key("sr%d_%d" % (base - 1, h), p.c("SR"))
        p.key("woev0", 0)
        p.key("woev1", 0)

        for j in range(NT):
            jp = j % 2
            p.wait("sync", "SY", sy_all)
            p.wait("sync", "TQ", p.key("womm%d" % (j - 2)))
            ysrc = y_h[:, :, j * 128:(j + 1) * 128].rearrange("c p t -> p c t")
            dma("sync", yTb[jp].rearrange("p (c t) -> p c t", c=16), ysrc, "LD")
            p.wait("sync", "SP", p.key("ffnsq%d" % (fst["use"] - 1)))
            p.wait("sync", "VP", p.key("x2v%d" % (j - 1)))
            dma("sync", dumpV[:, :], x1_h[j * 128:(j + 1) * 128, :], "LD")
            ld_snap = p.c("LD")
            p.wait("tensor", "LD", ld_snap)
            p.wait("tensor", "LW", lw_b2)
            if stage == "B2":
                p.wait("vector", "SOUT", 16 * (j - 1))
            else:
                p.wait("vector", "VP", p.key("q2f%d" % (fst["use"] - 2)))
            for q4 in range(4):
                bk = bank(4 + (q4 % 2))
                p.wait("tensor", "VP", p.key("woev%d" % (q4 % 2)))
                for kc in range(16):
                    p.op("tensor", lambda e, b=bk, k=kc, q=q4, yb=yTb[jp]:
                         e.matmul(b[:], yb[:, k * 128:(k + 1) * 128],
                                  woB[:, k * 2048 + q * 512: k * 2048 + (q + 1) * 512],
                                  start=(k == 0), stop=False))
                p.op("tensor", lambda e, b=bk, q=q4: e.matmul(
                    b[:], onesb[:], bout_bf[0:1, q * 512:(q + 1) * 512],
                    start=False, stop=True), inc="TQ")
                p.wait("vector", "TQ", p.c("TQ"))
                p.wait("vector", "LD", ld_snap)
                p.op("vector", lambda e, b=bk, q=q4, xbv=xb[jp]: e.tensor_add(
                    xbv[:, q * 512:(q + 1) * 512], b[:],
                    dumpV[:, q * 512:(q + 1) * 512]), inc="VP")
                p.key("woev%d" % (q4 % 2), p.c("VP"))
            p.key("womm%d" % j, p.c("TQ"))
            x2_ready = p.c("VP")
            p.key("x2v%d" % j, x2_ready)
            p.wait("vector", "VP", x2_ready)
            p.wait("scalar", "VP", x2_ready)
            if stage == "B2":
                p.key("ffnxq%d" % fst["use"], x2_ready)
                p.key("ffnsq%d" % fst["use"], 0)
                fst["use"] += 1
                p.wait("sync", "VP", x2_ready)
                p.wait("sync", "SOUT", 16 * (j - 1))
                dma("sync", out_h[j * 128:(j + 1) * 128, :], xb[jp], "SOUT")
                continue
            ffn_front(xb[jp], kqB)
            if j >= 1:
                vpr = ffn_back(fst["use"] - 2, ("SOUT", 16 * (j - 1)))
                p.wait("sync", "VP", vpr)
                dma("sync", out_h[(j - 1) * 128:j * 128, :], ybuf[:], "SOUT")
        if stage != "B2":
            vpr = ffn_back(fst["use"] - 1, ("SOUT", 16 * (NT - 1)))
            p.wait("sync", "VP", vpr)
            dma("sync", out_h[(NT - 1) * 128:NT * 128, :], ybuf[:], "SOUT")
        p.wait("sync", "SOUT", p.c("SOUT"))
        p.wait("gpsimd", "SKQ", skq_store)

        emit_into_block(nc, p, sem_handles)
    return nc


def emit_into_block(nc, p, sems):
    with nc.Block() as block:
        @block.sync
        def _(eng):
            p.replay("sync", eng, sems)

        @block.gpsimd
        def _(eng):
            p.replay("gpsimd", eng, sems)

        @block.tensor
        def _(eng):
            p.replay("tensor", eng, sems)

        @block.vector
        def _(eng):
            p.replay("vector", eng, sems)

        @block.scalar
        def _(eng):
            p.replay("scalar", eng, sems)


def _prep_inputs(inputs):
    def f32(a):
        return np.ascontiguousarray(np.asarray(a, dtype=np.float32))
    x = f32(inputs["L0_attn_x"]).reshape(16, 2048, 2048)
    base = {
        "wi": f32(inputs["L0_attn_W_in"]),
        "wd": f32(inputs["L0_attn_W_decay"]),
        "wg": f32(inputs["L0_attn_W_gate"]),
        "wo": f32(inputs["L0_attn_W_out"]),
        "wk": f32(inputs["L0_ffn_kernel"]),
        "bin": f32(inputs["L0_attn_b_in"]).reshape(2048),
        "bdec": f32(inputs["L0_attn_b_decay"]).reshape(2048),
        "bgate": f32(inputs["L0_attn_b_gate"]).reshape(2048),
        "bout": f32(inputs["L0_attn_b_out"]).reshape(1, 2048),
        "dbias": f32(inputs["L0_attn_decay_bias"]).reshape(2048),
        "fbias": f32(inputs["L0_ffn_bias"]).reshape(1, 2048),
        "eyef": np.eye(128, dtype=np.float32),
        "onesb": np.ones((1, 128), dtype=ml_dtypes.bfloat16),
    }
    in_maps = []
    for c in range(8):
        m = dict(base)
        m["x"] = np.ascontiguousarray(x[2 * c:2 * c + 2].reshape(NTOK, D))
        in_maps.append(m)
    return in_maps


def kernel(**inputs):
    in_maps = _prep_inputs(inputs)
    nc = build_nc("full")
    res = run_bass_kernel_spmd(nc, in_maps, core_ids=list(range(8)))
    outs = [np.asarray(r["out"], dtype=np.float32).reshape(2, 2048, 2048)
            for r in res.results]
    return np.concatenate(outs, axis=0)


# version 12
# speedup vs baseline: 1.3825x; 1.3825x over previous
import contextlib
import os
import numpy as np
import ml_dtypes
import concourse.bass as bass
import concourse.mybir as mybir
from concourse.bass_utils import run_bass_kernel_spmd

F32 = mybir.dt.float32
BF16 = mybir.dt.bfloat16
AX = mybir.AxisListType.X
OP = mybir.AluOpType
AF = mybir.ActivationFunctionType

D = 2048
NTOK = 4096
NT = 32
MAGIC = float(np.float32(1.5 * 2 ** 23))
EPS = 1e-5
INV_D = 1.0 / 2048.0
TINY = 1e-30

ENGS = ("sync", "gpsimd", "tensor", "vector", "scalar")


class Prog:
    def __init__(self):
        self.ops = {e: [] for e in ENGS}
        self.cnt = {}
        self.last_wait = {}
        self.keys = {}

    def sem(self, name):
        self.cnt[name] = 0

    def wait(self, eng, sem, val):
        if val is None or val <= 0:
            return
        key = (eng, sem)
        if self.last_wait.get(key, -1) >= val:
            return
        self.last_wait[key] = val
        self.ops[eng].append(("w", sem, val))

    def op(self, eng, fn, inc=None, n=1, nodrain=False):
        drain = eng in ("scalar", "vector") and not nodrain
        if inc is not None:
            self.cnt[inc] += n
            if drain:
                self.ops[eng].append(("o", fn))
                self.ops[eng].append(("i", lambda e: e.drain(), inc, n))
            else:
                self.ops[eng].append(("i", fn, inc, n))
        else:
            self.ops[eng].append(("o", fn))
            if drain:
                self.ops[eng].append(("o", lambda e: e.drain()))

    def sop(self, fn, inc=None, n=1):
        self.op("scalar", fn, inc=inc, n=n, nodrain=False)

    def c(self, sem):
        return self.cnt[sem]

    def key(self, name, val=None):
        if val is None:
            return self.keys.get(name, 0)
        self.keys[name] = val

    def replay(self, eng_name, eng, sems):
        for item in self.ops[eng_name]:
            if item[0] == "w":
                eng.wait_ge(sems[item[1]], item[2])
            elif item[0] == "i":
                item[1](eng).then_inc(sems[item[2]], item[3])
            else:
                item[1](eng)


def build_nc(stage="full"):
    nc = bass.Bass("TRN2", target_bir_lowering=False)

    x_h = nc.dram_tensor("x", [NTOK, D], F32, kind="ExternalInput").ap()
    wi_h = nc.dram_tensor("wi", [D, D], F32, kind="ExternalInput").ap()
    wd_h = nc.dram_tensor("wd", [D, D], F32, kind="ExternalInput").ap()
    wg_h = nc.dram_tensor("wg", [D, D], F32, kind="ExternalInput").ap()
    wo_h = nc.dram_tensor("wo", [D, D], F32, kind="ExternalInput").ap()
    wk_h = nc.dram_tensor("wk", [D, D], F32, kind="ExternalInput").ap()
    bin_h = nc.dram_tensor("bin", [D], F32, kind="ExternalInput").ap()
    bdec_h = nc.dram_tensor("bdec", [D], F32, kind="ExternalInput").ap()
    bgate_h = nc.dram_tensor("bgate", [D], F32, kind="ExternalInput").ap()
    bout_h = nc.dram_tensor("bout", [1, D], F32, kind="ExternalInput").ap()
    dbias_h = nc.dram_tensor("dbias", [D], F32, kind="ExternalInput").ap()
    fbias_h = nc.dram_tensor("fbias", [1, D], F32, kind="ExternalInput").ap()
    eyef_h = nc.dram_tensor("eyef", [128, 128], F32, kind="ExternalInput").ap()
    ones_h = nc.dram_tensor("onesb", [1, 128], BF16, kind="ExternalInput").ap()

    out_h = nc.dram_tensor("out", [NTOK, D], F32, kind="ExternalOutput").ap()
    x1_h = nc.dram_tensor("x1h", [NTOK, D], F32, kind="Internal").ap()
    y_h = nc.dram_tensor("yh", [16, 128, NTOK], BF16, kind="Internal").ap()
    kq_h = nc.dram_tensor("kqh", [128, 32768], BF16, kind="Internal").ap()

    p = Prog()
    for s in ("LD", "LW", "SX1", "SY", "SOUT", "SKQ", "TP", "VP", "SP",
              "TQ", "SR", "SEV", "VS", "TX"):
        p.sem(s)

    with contextlib.ExitStack() as es:
        ec = es.enter_context
        WB = ec(nc.sbuf_tensor("WB", [128, 75776], BF16))
        FA = ec(nc.sbuf_tensor("FA", [128, 12288], F32))
        statn = ["sums", "sumsq", "sumsq2", "m", "mm", "var", "vpe", "lnv",
                 "rstd", "nrstd", "nmr", "amax", "amcl", "rcp", "inv", "sx",
                 "amax2", "am2", "rcp2", "inv2", "alpha"]
        st2 = [
            {n: ec(nc.sbuf_tensor(f"st{q}_{n}", [128, 1], F32)) for n in statn}
            for q in range(2)
        ]
        stb = {n: ec(nc.sbuf_tensor("stb_" + n, [128, 1], F32))
               for n in ["sums", "sumsq", "sumsq2", "m", "mm", "var", "vpe",
                         "lnv", "nrstd"]}
        rstd_tab = ec(nc.sbuf_tensor("rstd_tab", [128, 32], F32))
        nmr_tab = ec(nc.sbuf_tensor("nmr_tab", [128, 32], F32))
        vpe_tab = ec(nc.sbuf_tensor("vpe_tab", [128, 32], F32))
        m_tab = ec(nc.sbuf_tensor("m_tab", [128, 32], F32))
        lnv_tab = ec(nc.sbuf_tensor("lnv_tab", [128, 32], F32))
        bin_c = ec(nc.sbuf_tensor("bin_c", [128, 16], F32))
        bgate_c = ec(nc.sbuf_tensor("bgate_c", [128, 16], F32))
        bdec_c = ec(nc.sbuf_tensor("bdec_c", [128, 16], F32))
        dbias_c = ec(nc.sbuf_tensor("dbias_c", [128, 16], F32))
        bd_c = ec(nc.sbuf_tensor("bd_c", [128, 16], F32))
        nbd_c = ec(nc.sbuf_tensor("nbd_c", [128, 16], F32))
        fbamax = ec(nc.sbuf_tensor("fbamax", [1, 1], F32))
        fbcl = ec(nc.sbuf_tensor("fbcl", [1, 1], F32))
        fbrcp = ec(nc.sbuf_tensor("fbrcp", [1, 1], F32))
        fbinv = ec(nc.sbuf_tensor("fbinv", [1, 1], F32))
        fbsk = ec(nc.sbuf_tensor("fbsk", [1, 1], F32))
        bq_bf = ec(nc.sbuf_tensor("bq_bf", [1, D], BF16))
        bout_bf = ec(nc.sbuf_tensor("bout_bf", [1, D], BF16))
        carry = ec(nc.sbuf_tensor("carry", [128, 32], F32))
        cmag = ec(nc.sbuf_tensor("cmag", [128, 1], F32))
        cneg = ec(nc.sbuf_tensor("cneg", [128, 1], F32))
        cnh = ec(nc.sbuf_tensor("cnh", [128, 1], F32))
        eyef = ec(nc.sbuf_tensor("eyef_s", [128, 128], F32))
        onesb = ec(nc.sbuf_tensor("onesb_s", [1, 128], BF16))
        pz = ec(nc.psum_tensor("pz", [128, 4096], F32))
        sem_handles = {nm: ec(nc.semaphore(nm)) for nm in p.cnt.keys()}

        def bank(i):
            return pz[:, i * 512:(i + 1) * 512]

        def bank2(i):
            return pz[:, i * 512:(i + 2) * 512]

        def wb(a, b):
            return WB[:, a:b]

        def fa(a, b):
            return FA[:, a:b]

        xb = [fa(0, 2048), fa(2048, 4096)]
        dumpV = fa(4096, 6144)
        dumpS = [fa(6144, 8192), fa(8192, 10240)]
        ybuf = fa(10240, 12288)
        d_stg = [fa(6144, 7168), fa(7168, 8192)]
        b_stg = [fa(8192, 9216), fa(9216, 10240)]
        s_stg = [fa(10240, 11264), fa(11264, 12288)]

        kqA = wb(0, 32768)
        yTb = [wb(65536, 67584), wb(67584, 69632)]
        qb_s = wb(69632, 71680)
        qbuf = [qb_s, qb_s]
        qT = [wb(71680, 73728), wb(73728, 75776)]
        yb16 = [qb_s, qb_s]
        kqTst = wb(45056, 47104)
        ln2T = [wb(0, 16384), wb(16384, 32768)]
        wslot = [wb(32768 + i * 2048, 32768 + (i + 1) * 2048) for i in range(6)]
        ln2b = [wb(45056, 47104), wb(47104, 49152)]
        tsig_st = wb(49152, 50176)
        z_st = wb(50176, 51200)
        inp_st = [wb(51200, 52224), wb(52224, 53248)]
        e_stg = [wb(53248, 54272), wb(54272, 55296)]
        gate_st = [wb(55296, 56320), wb(56320, 57344)]
        y_stg = [wb(57344, 58368), wb(58368, 59392)]
        woB = wb(0, 32768)
        kqB = wb(32768, 65536)

        def dma(eng, out, in_, inc, slow=False):
            if slow:
                p.op(eng, lambda e, o=out, i=in_: e.dma_start(
                    out=o, in_=i, allow_slow_non_contiguous=True), inc=inc, n=16)
            else:
                p.op(eng, lambda e, o=out, i=in_: e.dma_start(out=o, in_=i),
                     inc=inc, n=16)

        def tdma(out, in_):
            p.op("sync", lambda e, o=out, i=in_: e.dma_start(out=o, in_=i,
                                                             transpose=True),
                 inc="TX", n=16)

        p.op("vector", lambda e: e.memset(cmag[:], MAGIC))
        p.op("vector", lambda e: e.memset(cneg[:], -1.0))
        p.op("vector", lambda e: e.memset(cnh[:], -0.5))
        nc.const_aps.register(F32, MAGIC, cmag[:])
        nc.const_aps.register(F32, -1.0, cneg[:])
        nc.const_aps.register(F32, -0.5, cnh[:])
        dma("sync", eyef[:], eyef_h, "LD")
        dma("sync", onesb[:], ones_h, "LD")
        fbrow = FA[0:1, 4096:6144]
        fbt = FA[0:1, 6144:8192]
        dma("sync", fbrow, fbias_h, "LD")
        dma("gpsimd", bin_c[:], bin_h.rearrange("(c p) -> p c", p=128), "LW", slow=True)
        dma("gpsimd", bgate_c[:], bgate_h.rearrange("(c p) -> p c", p=128), "LW", slow=True)
        dma("gpsimd", bdec_c[:], bdec_h.rearrange("(c p) -> p c", p=128), "LW", slow=True)
        dma("gpsimd", dbias_c[:], dbias_h.rearrange("(c p) -> p c", p=128), "LW", slow=True)
        dma("gpsimd", bout_bf[:], bout_h, "LW")
        ld_consts = p.c("LD")
        lw_consts = p.c("LW")

        p.wait("vector", "LW", lw_consts)
        p.op("vector", lambda e: e.tensor_add(bd_c[:], bdec_c[:], dbias_c[:]))
        p.op("vector", lambda e: e.tensor_scalar(nbd_c[:], bd_c[:], -1.0, None,
                                                 op0=OP.mult))
        p.op("vector", lambda e: e.memset(carry[:], 0.0))
        p.wait("vector", "LD", ld_consts)
        p.op("vector", lambda e: e.reduce_max(fbamax[:], fbrow, axis=AX,
                                              apply_absolute_value=True))
        p.op("vector", lambda e: e.tensor_scalar(fbcl[:], fbamax[:], TINY, None,
                                                 op0=OP.max))
        p.op("vector", lambda e: e.reciprocal(fbrcp[:], fbcl[:]))
        p.op("vector", lambda e: e.tensor_scalar(fbinv[:], fbrcp[:], 127.0, None,
                                                 op0=OP.mult))
        p.op("vector", lambda e: e.tensor_scalar(fbsk[:], fbcl[:], 1.0 / 127.0,
                                                 None, op0=OP.mult))
        p.op("vector", lambda e: e.tensor_scalar(fbt, fbrow, fbinv[0:1, :], MAGIC,
                                                 op0=OP.mult, op1=OP.add))
        p.op("vector", lambda e: e.tensor_scalar(bq_bf[:], fbt, -MAGIC,
                                                 fbsk[0:1, :],
                                                 op0=OP.add, op1=OP.mult), inc="VP")

        st = st2[0]
        wkb = [qT[0], qT[1]]
        wkT = qbuf[0]
        for fc in range(16):
            buf = wkb[fc % 2]
            p.wait("gpsimd", "TX", p.key("kqld%d" % (fc - 2)))
            src_ap = wk_h[:, fc * 128:(fc + 1) * 128].rearrange("(k p) f -> p k f", p=128)
            dma("gpsimd", buf.rearrange("p (k f) -> p k f", k=16), src_ap, "LW")
            ld_snap = p.c("LW")
            p.wait("sync", "LW", ld_snap)
            p.wait("sync", "VP", p.key("kqq%d" % (fc - 1)))
            tdma(wkT.rearrange("p (c t) -> p c t", c=16), buf)
            p.key("kqld%d" % fc, p.c("TX"))
            p.wait("vector", "TX", p.c("TX"))
            p.op("vector", lambda e: e.reduce_max(st["amax"][:], wkT[:], axis=AX,
                                                  apply_absolute_value=True))
            p.op("vector", lambda e: e.tensor_scalar(st["amcl"][:], st["amax"][:],
                                                     TINY, None, op0=OP.max))
            p.op("vector", lambda e: e.reciprocal(st["rcp"][:], st["amcl"][:]))
            p.op("vector", lambda e: e.tensor_scalar(st["inv"][:], st["rcp"][:],
                                                     127.0, None, op0=OP.mult))
            p.op("vector", lambda e: e.tensor_scalar(st["sx"][:], st["amcl"][:],
                                                     1.0 / 127.0, None, op0=OP.mult))
            p.op("vector", lambda e: e.tensor_scalar(dumpS[0], wkT[:], st["inv"][:],
                                                     MAGIC, op0=OP.mult, op1=OP.add))
            p.op("vector", lambda e: e.tensor_scalar(kqTst[:], dumpS[0], -MAGIC,
                                                     st["sx"][:],
                                                     op0=OP.add, op1=OP.mult), inc="VP")
            p.key("kqq%d" % fc, p.c("VP"))
            p.wait("sync", "VP", p.c("VP"))
            kq3 = kqA.rearrange("p (k g) -> p k g", k=16)
            tdma(kq3[:, :, fc * 128:(fc + 1) * 128], kqTst[:])
            p.wait("vector", "TX", p.c("TX"))
        tx_kq_done = p.c("TX")
        prep_vp = p.c("VP")
        p.wait("gpsimd", "TX", tx_kq_done)
        dma("gpsimd", kq_h, kqA, "SKQ")
        skq_store = p.c("SKQ")

        fst = {"use": 0, "info": {}}

        def ffn_front(xsrc, kq_region, tab_j=None):
            u = fst["use"]
            fst["use"] = u + 1
            pr = u % 2
            st = st2[pr]
            fst["info"][u] = (xsrc, kq_region, tab_j)
            p.op("vector", lambda e: e.reduce_sum(st["sums"][:], xsrc, axis=AX))
            p.op("scalar", lambda e: e.activation(
                bank2(6), xsrc[:, 0:1024], AF.Square,
                accum_out=st["sumsq"][:]))
            p.op("scalar", lambda e: e.activation(
                bank2(6), xsrc[:, 1024:2048], AF.Square,
                accum_out=st["sumsq2"][:]), inc="SP")
            p.key("ffnsq%d" % u, p.c("SP"))
            p.wait("vector", "SP", p.c("SP"))
            p.op("vector", lambda e: e.tensor_add(st["sumsq"][:], st["sumsq"][:],
                                                  st["sumsq2"][:]))
            p.op("vector", lambda e: e.tensor_scalar(st["m"][:], st["sums"][:],
                                                     INV_D, None, op0=OP.mult))
            p.op("vector", lambda e: e.tensor_mul(st["mm"][:], st["m"][:], st["m"][:]))
            p.op("vector", lambda e: e.scalar_tensor_tensor(
                st["var"][:], st["sumsq"][:], INV_D, st["mm"][:],
                op0=OP.mult, op1=OP.subtract))
            p.op("vector", lambda e: e.tensor_scalar(st["vpe"][:], st["var"][:],
                                                     EPS, None, op0=OP.add), inc="VP")
            p.wait("scalar", "VP", p.c("VP"))
            p.sop(lambda e: e.activation(st["lnv"][:], st["vpe"][:], AF.Ln))
            p.sop(lambda e: e.activation(st["rstd"][:], st["lnv"][:],
                                         AF.Exp, scale=-0.5))
            p.sop(lambda e: e.activation(st["nrstd"][:], st["rstd"][:],
                                         AF.Identity, scale=-1.0))
            p.sop(lambda e: e.activation(st["nmr"][:], st["m"][:],
                                         AF.Identity, scale=st["nrstd"][:]))
            p.op("scalar", lambda e, pp=pr: e.activation(dumpS[pp], xsrc, AF.Identity,
                                                         bias=st["nmr"][:],
                                                         scale=st["rstd"][:]), inc="SP")
            p.wait("vector", "SP", p.c("SP"))
            p.op("vector", lambda e, pp=pr: e.reduce_max(st["amax"][:], dumpS[pp],
                                                         axis=AX,
                                                         apply_absolute_value=True))
            p.op("vector", lambda e: e.tensor_scalar(st["amcl"][:], st["amax"][:],
                                                     TINY, None, op0=OP.max))
            p.op("vector", lambda e: e.reciprocal(st["rcp"][:], st["amcl"][:]))
            p.op("vector", lambda e: e.tensor_scalar(st["inv"][:], st["rcp"][:],
                                                     127.0, None, op0=OP.mult))
            p.op("vector", lambda e: e.tensor_scalar(st["sx"][:], st["amcl"][:],
                                                     1.0 / 127.0, None, op0=OP.mult))
            p.op("vector", lambda e, pp=pr: e.tensor_scalar(
                dumpS[1 - pp], dumpS[pp], st["inv"][:], MAGIC,
                op0=OP.mult, op1=OP.add))
            p.wait("vector", "TX", p.key("tdma%d" % (u - 1)))
            p.op("vector", lambda e, pp=pr, q=qbuf[pr]: e.tensor_scalar(
                q, dumpS[1 - pp], -MAGIC, st["sx"][:], op0=OP.add, op1=OP.mult),
                inc="VP")
            vp_xq = p.c("VP")
            p.key("ffnxq%d" % u, vp_xq)
            p.wait("sync", "VP", vp_xq)
            p.wait("sync", "TQ", p.key("mmq%d" % (u - 2)))
            p.wait("sync", "VP", p.key("q2f%d" % (u - 2)))
            tdma(qT[pr].rearrange("p (c t) -> p c t", c=16), qbuf[pr])
            p.key("tdma%d" % u, p.c("TX"))

        def ffn_back(u, yb_gate):
            pr = u % 2
            st = st2[pr]
            xsrc, kq_region, tab_j = fst["info"].pop(u)
            p.wait("tensor", "TX", p.key("tdma%d" % u))
            for q4 in range(4):
                bk = bank(q4)
                p.wait("tensor", "SR", p.key("sr%d_%d" % (u - 1, q4 // 2)))
                for kc in range(16):
                    p.op("tensor", lambda e, b=bk, k=kc, jp=pr, q=q4, kr=kq_region:
                         e.matmul(b[:], qT[jp][:, k * 128:(k + 1) * 128],
                                  kr[:, k * 2048 + q * 512: k * 2048 + (q + 1) * 512],
                                  start=(k == 0), stop=False))
                p.op("tensor", lambda e, b=bk, q=q4: e.matmul(
                    b[:], onesb[:], bq_bf[0:1, q * 512:(q + 1) * 512],
                    start=False, stop=True), inc="TQ")
                if q4 % 2 == 1:
                    tq_snap = p.c("TQ")
                    p.wait("scalar", "TQ", tq_snap)
                    if yb_gate is not None:
                        p.wait("scalar", yb_gate[0], yb_gate[1])
                    h = q4 // 2
                    p.op("scalar", lambda e, hh=h: e.activation(
                        ybuf[:, hh * 1024:(hh + 1) * 1024], bank2(hh * 2), AF.Relu),
                        inc="SR")
                    p.key("sr%d_%d" % (u, h), p.c("SR"))
            p.key("mmq%d" % u, p.c("TQ"))
            p.wait("vector", "SR", p.c("SR"))
            p.op("vector", lambda e: e.reduce_max(st["amax2"][:], ybuf[:], axis=AX))
            p.op("vector", lambda e: e.tensor_scalar(st["am2"][:], st["amax2"][:],
                                                     TINY, None, op0=OP.max))
            p.op("vector", lambda e: e.reciprocal(st["rcp2"][:], st["am2"][:]))
            p.op("vector", lambda e: e.tensor_scalar(st["inv2"][:], st["rcp2"][:],
                                                     127.0, None, op0=OP.mult))
            p.op("vector", lambda e: e.tensor_scalar(st["alpha"][:], st["am2"][:],
                                                     0.5 / 127.0, None, op0=OP.mult))
            p.op("vector", lambda e, pp=pr: e.tensor_scalar(
                dumpS[pp], ybuf[:], st["inv2"][:], MAGIC, op0=OP.mult, op1=OP.add))
            p.wait("vector", "TQ", p.key("mmq%d" % u))
            p.op("vector", lambda e, q=qT[pr], pp=pr: e.tensor_scalar(
                q, dumpS[pp], -MAGIC, None, op0=OP.add))
            if yb_gate is not None:
                p.wait("vector", yb_gate[0], yb_gate[1])
            p.wait("vector", "SP", p.key("tabsq%d" % (u - 1)))
            p.op("vector", lambda e, q=qT[pr]: e.scalar_tensor_tensor(
                ybuf[:], q, st["alpha"][:], xsrc, op0=OP.mult, op1=OP.add),
                inc="VP")
            res_vp = p.c("VP")
            p.key("q2f%d" % u, res_vp)
            if tab_j is not None:
                p.op("vector", lambda e: e.reduce_sum(stb["sums"][:], ybuf[:],
                                                      axis=AX))
                p.wait("scalar", "VP", res_vp)
                p.op("scalar", lambda e: e.activation(
                    bank2(6), ybuf[:, 0:1024], AF.Square,
                    accum_out=stb["sumsq"][:]))
                p.op("scalar", lambda e: e.activation(
                    bank2(6), ybuf[:, 1024:2048], AF.Square,
                    accum_out=stb["sumsq2"][:]), inc="SP")
                p.key("tabsq%d" % u, p.c("SP"))
                p.wait("vector", "SP", p.c("SP"))
                p.op("vector", lambda e: e.tensor_add(stb["sumsq"][:],
                                                      stb["sumsq"][:],
                                                      stb["sumsq2"][:]))
                p.op("vector", lambda e, t=tab_j: e.tensor_scalar(
                    m_tab[:, t:t + 1], stb["sums"][:], INV_D, None, op0=OP.mult))
                p.op("vector", lambda e, t=tab_j: e.tensor_mul(
                    stb["mm"][:], m_tab[:, t:t + 1], m_tab[:, t:t + 1]))
                p.op("vector", lambda e, t=tab_j: e.scalar_tensor_tensor(
                    stb["var"][:], stb["sumsq"][:], INV_D, stb["mm"][:],
                    op0=OP.mult, op1=OP.subtract))
                p.op("vector", lambda e, t=tab_j: e.tensor_scalar(
                    vpe_tab[:, t:t + 1], stb["var"][:], EPS, None, op0=OP.add),
                    inc="VP")
            return res_vp

        p.wait("tensor", "VP", prep_vp)
        xb3 = [xb[0], xb[1], dumpV]
        for j in range(NT):
            if j < 3:
                p.wait("sync", "TX", tx_kq_done)
            p.wait("sync", "SX1", 16 * (j - 2))
            dma("sync", xb3[j % 3], x_h[j * 128:(j + 1) * 128, :], "LD")
            ld_snap = p.c("LD")
            p.wait("vector", "LD", ld_snap)
            p.wait("scalar", "LD", ld_snap)
            ffn_front(xb3[j % 3], kqA, tab_j=j)
            if j >= 1:
                vpr = ffn_back(fst["use"] - 2, ("SX1", 16 * (j - 1)))
                p.wait("sync", "VP", vpr)
                dma("sync", x1_h[(j - 1) * 128:j * 128, :], ybuf[:], "SX1")
        vpr = ffn_back(fst["use"] - 1, ("SX1", 16 * (NT - 1)))
        p.wait("sync", "VP", vpr)
        dma("sync", x1_h[(NT - 1) * 128:NT * 128, :], ybuf[:], "SX1")
        sx1_all = p.c("SX1")
        p.wait("scalar", "VP", p.c("VP"))
        p.sop(lambda e: e.activation(lnv_tab[:], vpe_tab[:], AF.Ln))
        p.op("scalar", lambda e: e.activation(rstd_tab[:], lnv_tab[:], AF.Exp,
                                              scale=-0.5), inc="SP")
        p.wait("vector", "SP", p.c("SP"))
        p.op("vector", lambda e: e.tensor_mul(lnv_tab[:], m_tab[:], rstd_tab[:]))
        p.op("vector", lambda e: e.tensor_scalar(nmr_tab[:], lnv_tab[:], -1.0,
                                                 None, op0=OP.mult), inc="VP")
        tabs_vp = p.c("VP")

        if stage == "A":
            o_tab = nc.dram_tensor("tabdump", [128, 64], F32,
                                   kind="ExternalOutput").ap()
            p.wait("sync", "SX1", sx1_all)
            p.wait("sync", "SP", p.c("SP"))
            dma("sync", out_h, x1_h, "SOUT")
            dma("sync", o_tab[:, 0:32], rstd_tab[:], "SOUT")
            dma("sync", o_tab[:, 32:64], nmr_tab[:], "SOUT")
            p.wait("sync", "SOUT", p.c("SOUT"))
            p.wait("gpsimd", "SKQ", skq_store)
            emit_into_block(nc, p, sem_handles)
            return nc

        wsrcs = [wi_h, wg_h, wd_h]

        def emit_wload(c):
            base = (c % 2) * 3
            for wi_, wsrc in enumerate(wsrcs):
                src = wsrc[:, c * 128:(c + 1) * 128].rearrange("(k p) f -> p k f", p=128)
                dst = wslot[base + wi_].rearrange("p (k f) -> p k f", k=16)
                dma("gpsimd", dst, src, "LW")
            p.key("wl%d" % (c % 2), p.c("LW"))

        emit_wload(0)
        emit_wload(1)

        p.wait("tensor", "SR", p.c("SR"))
        b1 = {"pmuse": 0}
        lnsnaps = []

        for s in range(4):
            tiles = [4 * s + i for i in range(4)] + [16 + 4 * s + i for i in range(4)]
            for i, tj in enumerate(tiles):
                gi = s * 8 + i
                p.wait("sync", "SX1", sx1_all)
                if gi >= 2:
                    p.wait("sync", "SP", lnsnaps[gi - 2])
                dma("sync", xb[i % 2], x1_h[tj * 128:(tj + 1) * 128, :], "LD")
                ld_snap = p.c("LD")
                p.wait("scalar", "LD", ld_snap)
                p.wait("scalar", "VP", tabs_vp)
                p.wait("scalar", "TX", p.key("b1tt%d" % (gi - 2)))
                p.op("scalar", lambda e, b=xb[i % 2], o=ln2b[i % 2], t=tj:
                     e.activation(o[:], b, AF.Identity,
                                  bias=nmr_tab[:, t:t + 1],
                                  scale=rstd_tab[:, t:t + 1]), inc="SP")
                lnsnaps.append(p.c("SP"))
                p.wait("sync", "SP", p.c("SP"))
                p.wait("sync", "TQ", p.key("b1mm%d" % (s - 2)))
                l3 = ln2T[s % 2].rearrange("p (k t) -> p k t", k=16)
                tdma(l3[:, :, i * 128:(i + 1) * 128], ln2b[i % 2][:])
                p.key("b1tt%d" % gi, p.c("TX"))
            tx_span_ready = p.c("TX")

            for c in range(16):
                gc = s * 16 + c
                cs = c % 2
                p.wait("tensor", "LW", p.key("wl%d" % (c % 2)))
                p.wait("tensor", "TX", tx_span_ready)
                banks = {}
                for wi_ in range(3):
                    for h in range(2):
                        u = b1["pmuse"]
                        bk = bank(u % 8)
                        p.wait("tensor", "SEV", p.key("pev%d" % (u % 8)))
                        b1["pmuse"] = u + 1
                        banks[(wi_, h)] = (bk, u)
                        for kc in range(16):
                            p.op("tensor",
                                 lambda e, b=bk, k=kc, w=wslot[(c % 2) * 3 + wi_],
                                 sp_=s, hh=h:
                                 e.matmul(b[:], w[:, k * 128:(k + 1) * 128],
                                          ln2T[sp_ % 2][:, k * 1024 + hh * 512:
                                                        k * 1024 + (hh + 1) * 512],
                                          start=(k == 0), stop=(k == 15)),
                                 inc=("TQ" if kc == 15 else None))
                        p.key("b1u%d" % u, p.c("TQ"))
                p.key("b1mm%d" % s, p.c("TQ"))
                tq_chunk = p.c("TQ")
                nxt = c + 2
                if nxt < 16:
                    p.wait("gpsimd", "TQ", tq_chunk)
                    emit_wload(nxt)
                elif s < 3:
                    p.wait("gpsimd", "TQ", tq_chunk)
                    emit_wload(nxt - 16)
                p.wait("scalar", "VS", p.key("b1v%d" % (gc - 1)))
                for h in range(2):
                    bk, u = banks[(0, h)]
                    p.wait("scalar", "TQ", p.key("b1u%d" % u))
                    p.op("scalar", lambda e, bb=bk, hh=h, bc=bin_c[:, c:c + 1]:
                         e.activation(tsig_st[:, hh * 512:(hh + 1) * 512], bb[:],
                                      AF.Sigmoid, bias=bc), inc="SP")
                    p.op("scalar", lambda e, bb=bk, hh=h, bc=bin_c[:, c:c + 1]:
                         e.activation(z_st[:, hh * 512:(hh + 1) * 512], bb[:],
                                      AF.Identity, bias=bc), inc="SEV")
                    p.key("pev%d" % (u % 8), p.c("SEV"))
                for h in range(2):
                    bk, u = banks[(1, h)]
                    p.wait("scalar", "TQ", p.key("b1u%d" % u))
                    p.op("scalar", lambda e, bb=bk, hh=h, bc=bgate_c[:, c:c + 1],
                         g=gate_st[cs]:
                         e.activation(g[:, hh * 512:(hh + 1) * 512], bb[:],
                                      AF.Sigmoid, bias=bc), inc="SEV")
                    p.key("pev%d" % (u % 8), p.c("SEV"))
                for h in range(2):
                    bk, u = banks[(2, h)]
                    p.wait("scalar", "TQ", p.key("b1u%d" % u))
                    p.op("scalar", lambda e, bb=bk, hh=h, bc=nbd_c[:, c:c + 1],
                         ee=e_stg[cs]:
                         e.activation(ee[:, hh * 512:(hh + 1) * 512], bb[:],
                                      AF.Sigmoid, bias=bc, scale=cneg[:]), inc="SP")
                    p.op("scalar", lambda e, bb=bk, hh=h, bc=bd_c[:, c:c + 1],
                         dd=d_stg[cs]:
                         e.activation(dd[:, hh * 512:(hh + 1) * 512], bb[:],
                                      AF.Sigmoid, bias=bc), inc="SEV")
                    p.key("pev%d" % (u % 8), p.c("SEV"))
                sev_chunk = p.c("SEV")
                sp_chunk = p.c("SP")
                p.wait("vector", "SEV", sev_chunk)
                p.wait("vector", "SP", sp_chunk)
                p.op("vector", lambda e, ii=inp_st[cs]: e.tensor_mul(ii[:], tsig_st[:],
                                                                     z_st[:]))
                p.op("vector", lambda e, ii=inp_st[cs], ee=e_stg[cs], bb=b_stg[cs]:
                     e.tensor_mul(bb[:], ee[:], ii[:]))
                for h in range(2):
                    p.op("vector", lambda e, dd=d_stg[cs], bb=b_stg[cs], ss=s_stg[cs],
                         hh=h, cc=c:
                         e.tensor_tensor_scan(ss[:, hh * 512:(hh + 1) * 512],
                                              dd[:, hh * 512:(hh + 1) * 512],
                                              bb[:, hh * 512:(hh + 1) * 512],
                                              carry[:, cc * 2 + hh:cc * 2 + hh + 1],
                                              op0=OP.mult, op1=OP.add))
                for h in range(2):
                    p.op("vector", lambda e, ss=s_stg[cs], hh=h, cc=c: e.tensor_copy(
                        carry[:, cc * 2 + hh:cc * 2 + hh + 1],
                        ss[:, hh * 512 + 511:hh * 512 + 512]))
                p.wait("vector", "SY", p.key("b1y%d" % (gc - 2)))
                p.op("vector", lambda e, g=gate_st[cs], ss=s_stg[cs], yy=y_stg[cs]:
                     e.tensor_mul(yy[:], g[:], ss[:]), inc="VS")
                p.key("b1v%d" % gc, p.c("VS"))
                p.wait("sync", "VS", p.c("VS"))
                ydst = y_h[c, :, :].rearrange("p (b t) -> p b t", b=2)[:, :,
                                                                      s * 512:(s + 1) * 512]
                dma("sync", ydst, y_stg[cs][:].rearrange("p (b t) -> p b t", b=2),
                    "SY")
                p.key("b1y%d" % gc, p.c("SY"))
        sy_all = p.c("SY")

        if stage == "B1":
            o_yd = nc.dram_tensor("ydump", [16, 128, NTOK], BF16,
                                  kind="ExternalOutput").ap()
            p.wait("gpsimd", "SY", sy_all)
            for cc in range(16):
                p.op("gpsimd", lambda e, c=cc: e.dma_start(
                    out=o_yd[c], in_=y_h[c]), inc="SOUT", n=16)
            p.wait("gpsimd", "SOUT", p.c("SOUT"))
            p.wait("gpsimd", "SKQ", skq_store)
            emit_into_block(nc, p, sem_handles)
            return nc

        p.wait("gpsimd", "TQ", p.c("TQ"))
        p.wait("gpsimd", "VS", p.c("VS"))
        src = wo_h.rearrange("(k p) f -> p k f", p=128)
        dma("gpsimd", woB.rearrange("p (k f) -> p k f", k=16), src, "LW")
        p.wait("gpsimd", "SKQ", skq_store)
        dma("gpsimd", kqB, kq_h, "LW")
        lw_b2 = p.c("LW")

        p.wait("tensor", "SEV", p.c("SEV"))
        fst["use"] += 2
        base = fst["use"]
        p.key("mmq%d" % (base - 2), p.c("TQ"))
        p.key("tdma%d" % (base - 1), p.c("TX"))
        for h in range(2):
            p.key("sr%d_%d" % (base - 1, h), p.c("SR"))
        p.key("woev0", 0)
        p.key("woev1", 0)

        def yt_load(j):
            p.wait("gpsimd", "TQ", p.key("womm%d" % (j - 2)))
            ysrc = y_h[:, :, j * 128:(j + 1) * 128].rearrange("c p t -> p c t")
            dma("gpsimd", yTb[j % 2].rearrange("p (c t) -> p c t", c=16),
                ysrc, "LW")
            p.key("ytld%d" % j, p.c("LW"))

        p.wait("gpsimd", "SY", sy_all)
        yt_load(0)

        for j in range(NT):
            jp = j % 2
            p.wait("sync", "SY", sy_all)
            p.wait("sync", "SP", p.key("ffnsq%d" % (fst["use"] - 1)))
            p.wait("sync", "VP", p.key("x2v%d" % (j - 1)))
            dma("sync", dumpV[:, :], x1_h[j * 128:(j + 1) * 128, :], "LD")
            ld_snap = p.c("LD")
            p.wait("tensor", "LD", ld_snap)
            p.wait("tensor", "LW", max(lw_b2, p.key("ytld%d" % j)))
            if stage == "B2":
                p.wait("vector", "SOUT", 16 * (j - 1))
            else:
                p.wait("vector", "VP", p.key("q2f%d" % (fst["use"] - 2)))
            for q4 in range(4):
                bk = bank(4 + (q4 % 2))
                p.wait("tensor", "VP", p.key("woev%d" % (q4 % 2)))
                for kc in range(16):
                    p.op("tensor", lambda e, b=bk, k=kc, q=q4, yb=yTb[jp]:
                         e.matmul(b[:], yb[:, k * 128:(k + 1) * 128],
                                  woB[:, k * 2048 + q * 512: k * 2048 + (q + 1) * 512],
                                  start=(k == 0), stop=False))
                p.op("tensor", lambda e, b=bk, q=q4: e.matmul(
                    b[:], onesb[:], bout_bf[0:1, q * 512:(q + 1) * 512],
                    start=False, stop=True), inc="TQ")
                p.wait("vector", "TQ", p.c("TQ"))
                p.wait("vector", "LD", ld_snap)
                p.op("vector", lambda e, b=bk, q=q4, xbv=xb[jp]: e.tensor_add(
                    xbv[:, q * 512:(q + 1) * 512], b[:],
                    dumpV[:, q * 512:(q + 1) * 512]), inc="VP")
                p.key("woev%d" % (q4 % 2), p.c("VP"))
            p.key("womm%d" % j, p.c("TQ"))
            if j + 1 < NT:
                yt_load(j + 1)
            x2_ready = p.c("VP")
            p.key("x2v%d" % j, x2_ready)
            p.wait("vector", "VP", x2_ready)
            p.wait("scalar", "VP", x2_ready)
            if stage == "B2":
                p.key("ffnxq%d" % fst["use"], x2_ready)
                p.key("ffnsq%d" % fst["use"], 0)
                fst["use"] += 1
                p.wait("sync", "VP", x2_ready)
                p.wait("sync", "SOUT", 16 * (j - 1))
                dma("sync", out_h[j * 128:(j + 1) * 128, :], xb[jp], "SOUT")
                continue
            ffn_front(xb[jp], kqB)
            if j >= 1:
                vpr = ffn_back(fst["use"] - 2, ("SOUT", 16 * (j - 1)))
                p.wait("sync", "VP", vpr)
                dma("sync", out_h[(j - 1) * 128:j * 128, :], ybuf[:], "SOUT")
        if stage != "B2":
            vpr = ffn_back(fst["use"] - 1, ("SOUT", 16 * (NT - 1)))
            p.wait("sync", "VP", vpr)
            dma("sync", out_h[(NT - 1) * 128:NT * 128, :], ybuf[:], "SOUT")
        p.wait("sync", "SOUT", p.c("SOUT"))
        p.wait("gpsimd", "SKQ", skq_store)

        emit_into_block(nc, p, sem_handles)
    return nc


def emit_into_block(nc, p, sems):
    with nc.Block() as block:
        @block.sync
        def _(eng):
            p.replay("sync", eng, sems)

        @block.gpsimd
        def _(eng):
            p.replay("gpsimd", eng, sems)

        @block.tensor
        def _(eng):
            p.replay("tensor", eng, sems)

        @block.vector
        def _(eng):
            p.replay("vector", eng, sems)

        @block.scalar
        def _(eng):
            p.replay("scalar", eng, sems)


def _prep_inputs(inputs):
    def f32(a):
        return np.ascontiguousarray(np.asarray(a, dtype=np.float32))
    x = f32(inputs["L0_attn_x"]).reshape(16, 2048, 2048)
    base = {
        "wi": f32(inputs["L0_attn_W_in"]),
        "wd": f32(inputs["L0_attn_W_decay"]),
        "wg": f32(inputs["L0_attn_W_gate"]),
        "wo": f32(inputs["L0_attn_W_out"]),
        "wk": f32(inputs["L0_ffn_kernel"]),
        "bin": f32(inputs["L0_attn_b_in"]).reshape(2048),
        "bdec": f32(inputs["L0_attn_b_decay"]).reshape(2048),
        "bgate": f32(inputs["L0_attn_b_gate"]).reshape(2048),
        "bout": f32(inputs["L0_attn_b_out"]).reshape(1, 2048),
        "dbias": f32(inputs["L0_attn_decay_bias"]).reshape(2048),
        "fbias": f32(inputs["L0_ffn_bias"]).reshape(1, 2048),
        "eyef": np.eye(128, dtype=np.float32),
        "onesb": np.ones((1, 128), dtype=ml_dtypes.bfloat16),
    }
    in_maps = []
    for c in range(8):
        m = dict(base)
        m["x"] = np.ascontiguousarray(x[2 * c:2 * c + 2].reshape(NTOK, D))
        in_maps.append(m)
    return in_maps


def kernel(**inputs):
    in_maps = _prep_inputs(inputs)
    nc = build_nc("full")
    res = run_bass_kernel_spmd(nc, in_maps, core_ids=list(range(8)))
    outs = [np.asarray(r["out"], dtype=np.float32).reshape(2, 2048, 2048)
            for r in res.results]
    return np.concatenate(outs, axis=0)


# version 13
# speedup vs baseline: 1.4582x; 1.0547x over previous
import contextlib
import os
import numpy as np
import ml_dtypes
import concourse.bass as bass
import concourse.mybir as mybir
from concourse.bass_utils import run_bass_kernel_spmd

F32 = mybir.dt.float32
BF16 = mybir.dt.bfloat16
AX = mybir.AxisListType.X
OP = mybir.AluOpType
AF = mybir.ActivationFunctionType

D = 2048
NTOK = 4096
NT = 32
MAGIC = float(np.float32(1.5 * 2 ** 23))
EPS = 1e-5
INV_D = 1.0 / 2048.0
TINY = 1e-30

ENGS = ("sync", "gpsimd", "tensor", "vector", "scalar")


class Prog:
    def __init__(self):
        self.ops = {e: [] for e in ENGS}
        self.cnt = {}
        self.last_wait = {}
        self.keys = {}

    def sem(self, name):
        self.cnt[name] = 0

    def wait(self, eng, sem, val):
        if val is None or val <= 0:
            return
        key = (eng, sem)
        if self.last_wait.get(key, -1) >= val:
            return
        self.last_wait[key] = val
        self.ops[eng].append(("w", sem, val))

    def op(self, eng, fn, inc=None, n=1, nodrain=False):
        drain = eng in ("scalar", "vector") and not nodrain
        if inc is not None:
            self.cnt[inc] += n
            if drain:
                self.ops[eng].append(("o", fn))
                self.ops[eng].append(("i", lambda e: e.drain(), inc, n))
            else:
                self.ops[eng].append(("i", fn, inc, n))
        else:
            self.ops[eng].append(("o", fn))
            if drain:
                self.ops[eng].append(("o", lambda e: e.drain()))

    def sop(self, fn, inc=None, n=1):
        self.op("scalar", fn, inc=inc, n=n, nodrain=False)

    def c(self, sem):
        return self.cnt[sem]

    def key(self, name, val=None):
        if val is None:
            return self.keys.get(name, 0)
        self.keys[name] = val

    def replay(self, eng_name, eng, sems):
        for item in self.ops[eng_name]:
            if item[0] == "w":
                eng.wait_ge(sems[item[1]], item[2])
            elif item[0] == "i":
                item[1](eng).then_inc(sems[item[2]], item[3])
            else:
                item[1](eng)


def build_nc(stage="full"):
    nc = bass.Bass("TRN2", target_bir_lowering=False)

    x_h = nc.dram_tensor("x", [NTOK, D], F32, kind="ExternalInput").ap()
    wi_h = nc.dram_tensor("wi", [D, D], F32, kind="ExternalInput").ap()
    wd_h = nc.dram_tensor("wd", [D, D], F32, kind="ExternalInput").ap()
    wg_h = nc.dram_tensor("wg", [D, D], F32, kind="ExternalInput").ap()
    wo_h = nc.dram_tensor("wo", [D, D], F32, kind="ExternalInput").ap()
    wk_h = nc.dram_tensor("wk", [D, D], F32, kind="ExternalInput").ap()
    bin_h = nc.dram_tensor("bin", [D], F32, kind="ExternalInput").ap()
    bdec_h = nc.dram_tensor("bdec", [D], F32, kind="ExternalInput").ap()
    bgate_h = nc.dram_tensor("bgate", [D], F32, kind="ExternalInput").ap()
    bout_h = nc.dram_tensor("bout", [1, D], F32, kind="ExternalInput").ap()
    dbias_h = nc.dram_tensor("dbias", [D], F32, kind="ExternalInput").ap()
    fbias_h = nc.dram_tensor("fbias", [1, D], F32, kind="ExternalInput").ap()
    eyef_h = nc.dram_tensor("eyef", [128, 128], F32, kind="ExternalInput").ap()
    ones_h = nc.dram_tensor("onesb", [1, 128], BF16, kind="ExternalInput").ap()

    out_h = nc.dram_tensor("out", [NTOK, D], F32, kind="ExternalOutput").ap()
    x1_h = nc.dram_tensor("x1h", [NTOK, D], F32, kind="Internal").ap()
    y_h = nc.dram_tensor("yh", [16, 128, NTOK], BF16, kind="Internal").ap()
    kq_h = nc.dram_tensor("kqh", [128, 32768], BF16, kind="Internal").ap()

    p = Prog()
    for s in ("LD", "LW", "SX1", "SY", "SOUT", "SKQ", "TP", "VP", "SP",
              "TQ", "SR", "SEV", "VS", "TX"):
        p.sem(s)

    with contextlib.ExitStack() as es:
        ec = es.enter_context
        WB = ec(nc.sbuf_tensor("WB", [128, 75776], BF16))
        FA = ec(nc.sbuf_tensor("FA", [128, 12288], F32))
        statn = ["sums", "sumsq", "sumsq2", "m", "mm", "var", "vpe", "lnv",
                 "rstd", "nrstd", "nmr", "amax", "amcl", "rcp", "inv", "sx",
                 "amax2", "am2", "rcp2", "inv2", "alpha"]
        st2 = [
            {n: ec(nc.sbuf_tensor(f"st{q}_{n}", [128, 1], F32)) for n in statn}
            for q in range(2)
        ]
        stb = {n: ec(nc.sbuf_tensor("stb_" + n, [128, 1], F32))
               for n in ["sums", "sumsq", "sumsq2", "m", "mm", "var", "vpe",
                         "lnv", "nrstd"]}
        rstd_tab = ec(nc.sbuf_tensor("rstd_tab", [128, 32], F32))
        nmr_tab = ec(nc.sbuf_tensor("nmr_tab", [128, 32], F32))
        vpe_tab = ec(nc.sbuf_tensor("vpe_tab", [128, 32], F32))
        m_tab = ec(nc.sbuf_tensor("m_tab", [128, 32], F32))
        lnv_tab = ec(nc.sbuf_tensor("lnv_tab", [128, 32], F32))
        bin_c = ec(nc.sbuf_tensor("bin_c", [128, 16], F32))
        bgate_c = ec(nc.sbuf_tensor("bgate_c", [128, 16], F32))
        bdec_c = ec(nc.sbuf_tensor("bdec_c", [128, 16], F32))
        dbias_c = ec(nc.sbuf_tensor("dbias_c", [128, 16], F32))
        bd_c = ec(nc.sbuf_tensor("bd_c", [128, 16], F32))
        nbd_c = ec(nc.sbuf_tensor("nbd_c", [128, 16], F32))
        fbamax = ec(nc.sbuf_tensor("fbamax", [1, 1], F32))
        fbcl = ec(nc.sbuf_tensor("fbcl", [1, 1], F32))
        fbrcp = ec(nc.sbuf_tensor("fbrcp", [1, 1], F32))
        fbinv = ec(nc.sbuf_tensor("fbinv", [1, 1], F32))
        fbsk = ec(nc.sbuf_tensor("fbsk", [1, 1], F32))
        bq_bf = ec(nc.sbuf_tensor("bq_bf", [1, D], BF16))
        bout_bf = ec(nc.sbuf_tensor("bout_bf", [1, D], BF16))
        carry = ec(nc.sbuf_tensor("carry", [128, 32], F32))
        cmag = ec(nc.sbuf_tensor("cmag", [128, 1], F32))
        cneg = ec(nc.sbuf_tensor("cneg", [128, 1], F32))
        cnh = ec(nc.sbuf_tensor("cnh", [128, 1], F32))
        eyef = ec(nc.sbuf_tensor("eyef_s", [128, 128], F32))
        onesb = ec(nc.sbuf_tensor("onesb_s", [1, 128], BF16))
        pz = ec(nc.psum_tensor("pz", [128, 4096], F32))
        sem_handles = {nm: ec(nc.semaphore(nm)) for nm in p.cnt.keys()}

        def bank(i):
            return pz[:, i * 512:(i + 1) * 512]

        def bank2(i):
            return pz[:, i * 512:(i + 2) * 512]

        def wb(a, b):
            return WB[:, a:b]

        def fa(a, b):
            return FA[:, a:b]

        xb = [fa(0, 2048), fa(2048, 4096)]
        dumpV = fa(4096, 6144)
        dumpS = [fa(6144, 8192), fa(8192, 10240)]
        ybuf = fa(10240, 12288)
        d_stg = [fa(6144, 7168), fa(7168, 8192)]
        b_stg = [fa(8192, 9216), fa(9216, 10240)]
        s_stg = [fa(10240, 11264), fa(11264, 12288)]

        kqA = wb(0, 32768)
        yTb = [wb(65536, 67584), wb(67584, 69632)]
        qb_s = wb(69632, 71680)
        qbuf = [qb_s, qb_s]
        qT = [wb(71680, 73728), wb(73728, 75776)]
        yb16 = [qb_s, qb_s]
        kqTst = wb(45056, 47104)
        ln2T = [wb(0, 16384), wb(16384, 32768)]
        wslot = [wb(32768 + i * 2048, 32768 + (i + 1) * 2048) for i in range(6)]
        ln2b = [wb(45056, 47104), wb(47104, 49152)]
        tsig_st = wb(49152, 50176)
        z_st = wb(50176, 51200)
        inp_st = [wb(51200, 52224), wb(52224, 53248)]
        e_stg = [wb(53248, 54272), wb(54272, 55296)]
        gate_st = [wb(55296, 56320), wb(56320, 57344)]
        y_stg = [wb(57344, 58368), wb(58368, 59392)]
        woB = wb(0, 32768)
        kqB = wb(32768, 65536)

        def dma(eng, out, in_, inc, slow=False):
            if slow:
                p.op(eng, lambda e, o=out, i=in_: e.dma_start(
                    out=o, in_=i, allow_slow_non_contiguous=True), inc=inc, n=16)
            else:
                p.op(eng, lambda e, o=out, i=in_: e.dma_start(out=o, in_=i),
                     inc=inc, n=16)

        def tdma(out, in_):
            p.op("sync", lambda e, o=out, i=in_: e.dma_start(out=o, in_=i,
                                                             transpose=True),
                 inc="TX", n=16)

        p.op("vector", lambda e: e.memset(cmag[:], MAGIC))
        p.op("vector", lambda e: e.memset(cneg[:], -1.0))
        p.op("vector", lambda e: e.memset(cnh[:], -0.5))
        nc.const_aps.register(F32, MAGIC, cmag[:])
        nc.const_aps.register(F32, -1.0, cneg[:])
        nc.const_aps.register(F32, -0.5, cnh[:])
        dma("sync", eyef[:], eyef_h, "LD")
        dma("sync", onesb[:], ones_h, "LD")
        fbrow = FA[0:1, 4096:6144]
        fbt = FA[0:1, 6144:8192]
        dma("sync", fbrow, fbias_h, "LD")
        dma("gpsimd", bin_c[:], bin_h.rearrange("(c p) -> p c", p=128), "LW", slow=True)
        dma("gpsimd", bgate_c[:], bgate_h.rearrange("(c p) -> p c", p=128), "LW", slow=True)
        dma("gpsimd", bdec_c[:], bdec_h.rearrange("(c p) -> p c", p=128), "LW", slow=True)
        dma("gpsimd", dbias_c[:], dbias_h.rearrange("(c p) -> p c", p=128), "LW", slow=True)
        dma("gpsimd", bout_bf[:], bout_h, "LW")
        ld_consts = p.c("LD")
        lw_consts = p.c("LW")

        p.wait("vector", "LW", lw_consts)
        p.op("vector", lambda e: e.tensor_add(bd_c[:], bdec_c[:], dbias_c[:]))
        p.op("vector", lambda e: e.tensor_scalar(nbd_c[:], bd_c[:], -1.0, None,
                                                 op0=OP.mult))
        p.op("vector", lambda e: e.memset(carry[:], 0.0))
        p.wait("vector", "LD", ld_consts)
        p.op("vector", lambda e: e.reduce_max(fbamax[:], fbrow, axis=AX,
                                              apply_absolute_value=True))
        p.op("vector", lambda e: e.tensor_scalar(fbcl[:], fbamax[:], TINY, None,
                                                 op0=OP.max))
        p.op("vector", lambda e: e.reciprocal(fbrcp[:], fbcl[:]))
        p.op("vector", lambda e: e.tensor_scalar(fbinv[:], fbrcp[:], 127.0, None,
                                                 op0=OP.mult))
        p.op("vector", lambda e: e.tensor_scalar(fbsk[:], fbcl[:], 1.0 / 127.0,
                                                 None, op0=OP.mult))
        p.op("vector", lambda e: e.tensor_scalar(fbt, fbrow, fbinv[0:1, :], MAGIC,
                                                 op0=OP.mult, op1=OP.add))
        p.op("vector", lambda e: e.tensor_scalar(bq_bf[:], fbt, -MAGIC,
                                                 fbsk[0:1, :],
                                                 op0=OP.add, op1=OP.mult), inc="VP")

        st = st2[0]
        wkb = [qT[0], qT[1]]
        wkT = qbuf[0]
        for fc in range(16):
            buf = wkb[fc % 2]
            p.wait("gpsimd", "TX", p.key("kqld%d" % (fc - 2)))
            src_ap = wk_h[:, fc * 128:(fc + 1) * 128].rearrange("(k p) f -> p k f", p=128)
            dma("gpsimd", buf.rearrange("p (k f) -> p k f", k=16), src_ap, "LW")
            ld_snap = p.c("LW")
            p.wait("sync", "LW", ld_snap)
            p.wait("sync", "VP", p.key("kqq%d" % (fc - 1)))
            tdma(wkT.rearrange("p (c t) -> p c t", c=16), buf)
            p.key("kqld%d" % fc, p.c("TX"))
            p.wait("vector", "TX", p.c("TX"))
            p.op("vector", lambda e: e.reduce_max(st["amax"][:], wkT[:], axis=AX,
                                                  apply_absolute_value=True))
            p.op("vector", lambda e: e.tensor_scalar(st["amcl"][:], st["amax"][:],
                                                     TINY, None, op0=OP.max))
            p.op("vector", lambda e: e.reciprocal(st["rcp"][:], st["amcl"][:]))
            p.op("vector", lambda e: e.tensor_scalar(st["inv"][:], st["rcp"][:],
                                                     127.0, None, op0=OP.mult))
            p.op("vector", lambda e: e.tensor_scalar(st["sx"][:], st["amcl"][:],
                                                     1.0 / 127.0, None, op0=OP.mult))
            p.op("vector", lambda e: e.tensor_scalar(dumpS[0], wkT[:], st["inv"][:],
                                                     MAGIC, op0=OP.mult, op1=OP.add))
            p.op("vector", lambda e: e.tensor_scalar(kqTst[:], dumpS[0], -MAGIC,
                                                     st["sx"][:],
                                                     op0=OP.add, op1=OP.mult), inc="VP")
            p.key("kqq%d" % fc, p.c("VP"))
            p.wait("sync", "VP", p.c("VP"))
            kq3 = kqA.rearrange("p (k g) -> p k g", k=16)
            tdma(kq3[:, :, fc * 128:(fc + 1) * 128], kqTst[:])
            p.wait("vector", "TX", p.c("TX"))
        tx_kq_done = p.c("TX")
        prep_vp = p.c("VP")
        p.wait("gpsimd", "TX", tx_kq_done)
        dma("gpsimd", kq_h, kqA, "SKQ")
        skq_store = p.c("SKQ")

        fst = {"use": 0, "info": {}}

        def ffn_front(xsrc, kq_region, tab_j=None):
            u = fst["use"]
            fst["use"] = u + 1
            pr = u % 2
            st = st2[pr]
            fst["info"][u] = (xsrc, kq_region, tab_j)
            p.op("vector", lambda e: e.reduce_sum(st["sums"][:], xsrc, axis=AX))
            p.op("scalar", lambda e: e.activation(
                bank2(6), xsrc[:, 0:1024], AF.Square,
                accum_out=st["sumsq"][:]))
            p.op("scalar", lambda e: e.activation(
                bank2(6), xsrc[:, 1024:2048], AF.Square,
                accum_out=st["sumsq2"][:]), inc="SP")
            p.key("ffnsq%d" % u, p.c("SP"))
            p.wait("vector", "SP", p.c("SP"))
            p.op("vector", lambda e: e.tensor_add(st["sumsq"][:], st["sumsq"][:],
                                                  st["sumsq2"][:]))
            p.op("vector", lambda e: e.tensor_scalar(st["m"][:], st["sums"][:],
                                                     INV_D, None, op0=OP.mult))
            p.op("vector", lambda e: e.tensor_mul(st["mm"][:], st["m"][:], st["m"][:]))
            p.op("vector", lambda e: e.scalar_tensor_tensor(
                st["var"][:], st["sumsq"][:], INV_D, st["mm"][:],
                op0=OP.mult, op1=OP.subtract))
            p.op("vector", lambda e: e.tensor_scalar(st["vpe"][:], st["var"][:],
                                                     EPS, None, op0=OP.add), inc="VP")
            p.wait("scalar", "VP", p.c("VP"))
            p.sop(lambda e: e.activation(st["lnv"][:], st["vpe"][:], AF.Ln))
            p.sop(lambda e: e.activation(st["rstd"][:], st["lnv"][:],
                                         AF.Exp, scale=-0.5))
            p.sop(lambda e: e.activation(st["nrstd"][:], st["rstd"][:],
                                         AF.Identity, scale=-1.0))
            p.sop(lambda e: e.activation(st["nmr"][:], st["m"][:],
                                         AF.Identity, scale=st["nrstd"][:]))
            p.op("scalar", lambda e, pp=pr: e.activation(dumpS[pp], xsrc, AF.Identity,
                                                         bias=st["nmr"][:],
                                                         scale=st["rstd"][:]), inc="SP")
            p.wait("vector", "SP", p.c("SP"))
            p.op("vector", lambda e, pp=pr: e.reduce_max(st["amax"][:], dumpS[pp],
                                                         axis=AX,
                                                         apply_absolute_value=True))
            p.op("vector", lambda e: e.tensor_scalar(st["amcl"][:], st["amax"][:],
                                                     TINY, None, op0=OP.max))
            p.op("vector", lambda e: e.reciprocal(st["rcp"][:], st["amcl"][:]))
            p.op("vector", lambda e: e.tensor_scalar(st["inv"][:], st["rcp"][:],
                                                     127.0, None, op0=OP.mult))
            p.op("vector", lambda e: e.tensor_scalar(st["sx"][:], st["amcl"][:],
                                                     1.0 / 127.0, None, op0=OP.mult))
            p.op("vector", lambda e, pp=pr: e.tensor_scalar(
                dumpS[1 - pp], dumpS[pp], st["inv"][:], MAGIC,
                op0=OP.mult, op1=OP.add))
            p.wait("vector", "TX", p.key("tdma%d" % (u - 1)))
            p.op("vector", lambda e, pp=pr, q=qbuf[pr]: e.tensor_scalar(
                q, dumpS[1 - pp], -MAGIC, st["sx"][:], op0=OP.add, op1=OP.mult),
                inc="VP")
            vp_xq = p.c("VP")
            p.key("ffnxq%d" % u, vp_xq)
            p.wait("sync", "VP", vp_xq)
            p.wait("sync", "TQ", p.key("mmq%d" % (u - 2)))
            p.wait("sync", "VP", p.key("q2f%d" % (u - 2)))
            tdma(qT[pr].rearrange("p (c t) -> p c t", c=16), qbuf[pr])
            p.key("tdma%d" % u, p.c("TX"))

        def ffn_back(u, yb_gate):
            pr = u % 2
            st = st2[pr]
            xsrc, kq_region, tab_j = fst["info"].pop(u)
            p.wait("tensor", "TX", p.key("tdma%d" % u))
            for q4 in range(4):
                bk = bank(q4)
                p.wait("tensor", "SR", p.key("sr%d_%d" % (u - 1, q4 // 2)))
                for kc in range(16):
                    p.op("tensor", lambda e, b=bk, k=kc, jp=pr, q=q4, kr=kq_region:
                         e.matmul(b[:], qT[jp][:, k * 128:(k + 1) * 128],
                                  kr[:, k * 2048 + q * 512: k * 2048 + (q + 1) * 512],
                                  start=(k == 0), stop=False))
                p.op("tensor", lambda e, b=bk, q=q4: e.matmul(
                    b[:], onesb[:], bq_bf[0:1, q * 512:(q + 1) * 512],
                    start=False, stop=True), inc="TQ")
                if q4 % 2 == 1:
                    tq_snap = p.c("TQ")
                    p.wait("scalar", "TQ", tq_snap)
                    if yb_gate is not None:
                        p.wait("scalar", yb_gate[0], yb_gate[1])
                    h = q4 // 2
                    p.op("scalar", lambda e, hh=h: e.activation(
                        ybuf[:, hh * 1024:(hh + 1) * 1024], bank2(hh * 2), AF.Relu),
                        inc="SR")
                    p.key("sr%d_%d" % (u, h), p.c("SR"))
            p.key("mmq%d" % u, p.c("TQ"))
            p.wait("vector", "SR", p.c("SR"))
            p.op("vector", lambda e: e.reduce_max(st["amax2"][:], ybuf[:], axis=AX))
            p.op("vector", lambda e: e.tensor_scalar(st["am2"][:], st["amax2"][:],
                                                     TINY, None, op0=OP.max))
            p.op("vector", lambda e: e.reciprocal(st["rcp2"][:], st["am2"][:]))
            p.op("vector", lambda e: e.tensor_scalar(st["inv2"][:], st["rcp2"][:],
                                                     127.0, None, op0=OP.mult))
            p.op("vector", lambda e: e.tensor_scalar(st["alpha"][:], st["am2"][:],
                                                     0.5 / 127.0, None, op0=OP.mult))
            p.op("vector", lambda e, pp=pr: e.tensor_scalar(
                dumpS[pp], ybuf[:], st["inv2"][:], MAGIC, op0=OP.mult, op1=OP.add))
            p.wait("vector", "TQ", p.key("mmq%d" % u))
            p.op("vector", lambda e, q=qT[pr], pp=pr: e.tensor_scalar(
                q, dumpS[pp], -MAGIC, None, op0=OP.add))
            if yb_gate is not None:
                p.wait("vector", yb_gate[0], yb_gate[1])
            p.wait("vector", "SP", p.key("tabsq%d" % (u - 1)))
            p.op("vector", lambda e, q=qT[pr]: e.scalar_tensor_tensor(
                ybuf[:], q, st["alpha"][:], xsrc, op0=OP.mult, op1=OP.add),
                inc="VP")
            res_vp = p.c("VP")
            p.key("q2f%d" % u, res_vp)
            if tab_j is not None:
                p.op("vector", lambda e: e.reduce_sum(stb["sums"][:], ybuf[:],
                                                      axis=AX))
                p.wait("scalar", "VP", res_vp)
                p.op("scalar", lambda e: e.activation(
                    bank2(6), ybuf[:, 0:1024], AF.Square,
                    accum_out=stb["sumsq"][:]))
                p.op("scalar", lambda e: e.activation(
                    bank2(6), ybuf[:, 1024:2048], AF.Square,
                    accum_out=stb["sumsq2"][:]), inc="SP")
                p.key("tabsq%d" % u, p.c("SP"))
                p.wait("vector", "SP", p.c("SP"))
                p.op("vector", lambda e: e.tensor_add(stb["sumsq"][:],
                                                      stb["sumsq"][:],
                                                      stb["sumsq2"][:]))
                p.op("vector", lambda e, t=tab_j: e.tensor_scalar(
                    m_tab[:, t:t + 1], stb["sums"][:], INV_D, None, op0=OP.mult))
                p.op("vector", lambda e, t=tab_j: e.tensor_mul(
                    stb["mm"][:], m_tab[:, t:t + 1], m_tab[:, t:t + 1]))
                p.op("vector", lambda e, t=tab_j: e.scalar_tensor_tensor(
                    stb["var"][:], stb["sumsq"][:], INV_D, stb["mm"][:],
                    op0=OP.mult, op1=OP.subtract))
                p.op("vector", lambda e, t=tab_j: e.tensor_scalar(
                    vpe_tab[:, t:t + 1], stb["var"][:], EPS, None, op0=OP.add),
                    inc="VP")
            return res_vp

        p.wait("tensor", "VP", prep_vp)
        xb3 = [xb[0], xb[1], dumpV]
        xld = {}

        def x_load(j):
            if j >= 3:
                p.wait("gpsimd", "VP", p.key("q2f%d" % (j - 3)))
            dma("gpsimd", xb3[j % 3], x_h[j * 128:(j + 1) * 128, :], "LW")
            xld[j] = p.c("LW")

        for j in range(3):
            x_load(j)
        for j in range(NT):
            if j == 0:
                p.wait("sync", "TX", tx_kq_done)
            p.wait("vector", "LW", xld[j])
            p.wait("scalar", "LW", xld[j])
            ffn_front(xb3[j % 3], kqA, tab_j=j)
            if j >= 1:
                vpr = ffn_back(fst["use"] - 2, ("SX1", 16 * (j - 1)))
                if j + 2 < NT:
                    x_load(j + 2)
                p.wait("sync", "VP", vpr)
                dma("sync", x1_h[(j - 1) * 128:j * 128, :], ybuf[:], "SX1")
        vpr = ffn_back(fst["use"] - 1, ("SX1", 16 * (NT - 1)))
        p.wait("sync", "VP", vpr)
        dma("sync", x1_h[(NT - 1) * 128:NT * 128, :], ybuf[:], "SX1")
        sx1_all = p.c("SX1")
        p.wait("scalar", "VP", p.c("VP"))
        p.sop(lambda e: e.activation(lnv_tab[:], vpe_tab[:], AF.Ln))
        p.op("scalar", lambda e: e.activation(rstd_tab[:], lnv_tab[:], AF.Exp,
                                              scale=-0.5), inc="SP")
        p.wait("vector", "SP", p.c("SP"))
        p.op("vector", lambda e: e.tensor_mul(lnv_tab[:], m_tab[:], rstd_tab[:]))
        p.op("vector", lambda e: e.tensor_scalar(nmr_tab[:], lnv_tab[:], -1.0,
                                                 None, op0=OP.mult), inc="VP")
        tabs_vp = p.c("VP")

        if stage == "A":
            o_tab = nc.dram_tensor("tabdump", [128, 64], F32,
                                   kind="ExternalOutput").ap()
            p.wait("sync", "SX1", sx1_all)
            p.wait("sync", "SP", p.c("SP"))
            dma("sync", out_h, x1_h, "SOUT")
            dma("sync", o_tab[:, 0:32], rstd_tab[:], "SOUT")
            dma("sync", o_tab[:, 32:64], nmr_tab[:], "SOUT")
            p.wait("sync", "SOUT", p.c("SOUT"))
            p.wait("gpsimd", "SKQ", skq_store)
            emit_into_block(nc, p, sem_handles)
            return nc

        wsrcs = [wi_h, wg_h, wd_h]

        def emit_wload(c):
            base = (c % 2) * 3
            for wi_, wsrc in enumerate(wsrcs):
                src = wsrc[:, c * 128:(c + 1) * 128].rearrange("(k p) f -> p k f", p=128)
                dst = wslot[base + wi_].rearrange("p (k f) -> p k f", k=16)
                dma("gpsimd", dst, src, "LW")
            p.key("wl%d" % (c % 2), p.c("LW"))

        emit_wload(0)
        emit_wload(1)

        p.wait("tensor", "SR", p.c("SR"))
        b1 = {"pmuse": 0}
        lnsnaps = []

        for s in range(4):
            tiles = [4 * s + i for i in range(4)] + [16 + 4 * s + i for i in range(4)]
            for i, tj in enumerate(tiles):
                gi = s * 8 + i
                p.wait("sync", "SX1", sx1_all)
                if gi >= 2:
                    p.wait("sync", "SP", lnsnaps[gi - 2])
                dma("sync", xb[i % 2], x1_h[tj * 128:(tj + 1) * 128, :], "LD")
                ld_snap = p.c("LD")
                p.wait("scalar", "LD", ld_snap)
                p.wait("scalar", "VP", tabs_vp)
                p.wait("scalar", "TX", p.key("b1tt%d" % (gi - 2)))
                p.op("scalar", lambda e, b=xb[i % 2], o=ln2b[i % 2], t=tj:
                     e.activation(o[:], b, AF.Identity,
                                  bias=nmr_tab[:, t:t + 1],
                                  scale=rstd_tab[:, t:t + 1]), inc="SP")
                lnsnaps.append(p.c("SP"))
                p.wait("sync", "SP", p.c("SP"))
                p.wait("sync", "TQ", p.key("b1mm%d" % (s - 2)))
                l3 = ln2T[s % 2].rearrange("p (k t) -> p k t", k=16)
                tdma(l3[:, :, i * 128:(i + 1) * 128], ln2b[i % 2][:])
                p.key("b1tt%d" % gi, p.c("TX"))
            tx_span_ready = p.c("TX")

            for c in range(16):
                gc = s * 16 + c
                cs = c % 2
                p.wait("tensor", "LW", p.key("wl%d" % (c % 2)))
                p.wait("tensor", "TX", tx_span_ready)
                banks = {}
                for wi_ in range(3):
                    for h in range(2):
                        u = b1["pmuse"]
                        bk = bank(u % 8)
                        p.wait("tensor", "SEV", p.key("pev%d" % (u % 8)))
                        b1["pmuse"] = u + 1
                        banks[(wi_, h)] = (bk, u)
                        for kc in range(16):
                            p.op("tensor",
                                 lambda e, b=bk, k=kc, w=wslot[(c % 2) * 3 + wi_],
                                 sp_=s, hh=h:
                                 e.matmul(b[:], w[:, k * 128:(k + 1) * 128],
                                          ln2T[sp_ % 2][:, k * 1024 + hh * 512:
                                                        k * 1024 + (hh + 1) * 512],
                                          start=(k == 0), stop=(k == 15)),
                                 inc=("TQ" if kc == 15 else None))
                        p.key("b1u%d" % u, p.c("TQ"))
                p.key("b1mm%d" % s, p.c("TQ"))
                tq_chunk = p.c("TQ")
                nxt = c + 2
                if nxt < 16:
                    p.wait("gpsimd", "TQ", tq_chunk)
                    emit_wload(nxt)
                elif s < 3:
                    p.wait("gpsimd", "TQ", tq_chunk)
                    emit_wload(nxt - 16)
                p.wait("scalar", "VS", p.key("b1v%d" % (gc - 1)))
                for h in range(2):
                    bk, u = banks[(0, h)]
                    p.wait("scalar", "TQ", p.key("b1u%d" % u))
                    p.op("scalar", lambda e, bb=bk, hh=h, bc=bin_c[:, c:c + 1]:
                         e.activation(tsig_st[:, hh * 512:(hh + 1) * 512], bb[:],
                                      AF.Sigmoid, bias=bc), inc="SP")
                    p.op("scalar", lambda e, bb=bk, hh=h, bc=bin_c[:, c:c + 1]:
                         e.activation(z_st[:, hh * 512:(hh + 1) * 512], bb[:],
                                      AF.Identity, bias=bc), inc="SEV")
                    p.key("pev%d" % (u % 8), p.c("SEV"))
                for h in range(2):
                    bk, u = banks[(1, h)]
                    p.wait("scalar", "TQ", p.key("b1u%d" % u))
                    p.op("scalar", lambda e, bb=bk, hh=h, bc=bgate_c[:, c:c + 1],
                         g=gate_st[cs]:
                         e.activation(g[:, hh * 512:(hh + 1) * 512], bb[:],
                                      AF.Sigmoid, bias=bc), inc="SEV")
                    p.key("pev%d" % (u % 8), p.c("SEV"))
                for h in range(2):
                    bk, u = banks[(2, h)]
                    p.wait("scalar", "TQ", p.key("b1u%d" % u))
                    p.op("scalar", lambda e, bb=bk, hh=h, bc=nbd_c[:, c:c + 1],
                         ee=e_stg[cs]:
                         e.activation(ee[:, hh * 512:(hh + 1) * 512], bb[:],
                                      AF.Sigmoid, bias=bc, scale=cneg[:]), inc="SP")
                    p.op("scalar", lambda e, bb=bk, hh=h, bc=bd_c[:, c:c + 1],
                         dd=d_stg[cs]:
                         e.activation(dd[:, hh * 512:(hh + 1) * 512], bb[:],
                                      AF.Sigmoid, bias=bc), inc="SEV")
                    p.key("pev%d" % (u % 8), p.c("SEV"))
                sev_chunk = p.c("SEV")
                sp_chunk = p.c("SP")
                p.wait("vector", "SEV", sev_chunk)
                p.wait("vector", "SP", sp_chunk)
                p.op("vector", lambda e, ii=inp_st[cs]: e.tensor_mul(ii[:], tsig_st[:],
                                                                     z_st[:]))
                p.op("vector", lambda e, ii=inp_st[cs], ee=e_stg[cs], bb=b_stg[cs]:
                     e.tensor_mul(bb[:], ee[:], ii[:]))
                for h in range(2):
                    p.op("vector", lambda e, dd=d_stg[cs], bb=b_stg[cs], ss=s_stg[cs],
                         hh=h, cc=c:
                         e.tensor_tensor_scan(ss[:, hh * 512:(hh + 1) * 512],
                                              dd[:, hh * 512:(hh + 1) * 512],
                                              bb[:, hh * 512:(hh + 1) * 512],
                                              carry[:, cc * 2 + hh:cc * 2 + hh + 1],
                                              op0=OP.mult, op1=OP.add))
                for h in range(2):
                    p.op("vector", lambda e, ss=s_stg[cs], hh=h, cc=c: e.tensor_copy(
                        carry[:, cc * 2 + hh:cc * 2 + hh + 1],
                        ss[:, hh * 512 + 511:hh * 512 + 512]))
                p.wait("vector", "SY", p.key("b1y%d" % (gc - 2)))
                p.op("vector", lambda e, g=gate_st[cs], ss=s_stg[cs], yy=y_stg[cs]:
                     e.tensor_mul(yy[:], g[:], ss[:]), inc="VS")
                p.key("b1v%d" % gc, p.c("VS"))
                p.wait("sync", "VS", p.c("VS"))
                ydst = y_h[c, :, :].rearrange("p (b t) -> p b t", b=2)[:, :,
                                                                      s * 512:(s + 1) * 512]
                dma("sync", ydst, y_stg[cs][:].rearrange("p (b t) -> p b t", b=2),
                    "SY")
                p.key("b1y%d" % gc, p.c("SY"))
        sy_all = p.c("SY")

        if stage == "B1":
            o_yd = nc.dram_tensor("ydump", [16, 128, NTOK], BF16,
                                  kind="ExternalOutput").ap()
            p.wait("gpsimd", "SY", sy_all)
            for cc in range(16):
                p.op("gpsimd", lambda e, c=cc: e.dma_start(
                    out=o_yd[c], in_=y_h[c]), inc="SOUT", n=16)
            p.wait("gpsimd", "SOUT", p.c("SOUT"))
            p.wait("gpsimd", "SKQ", skq_store)
            emit_into_block(nc, p, sem_handles)
            return nc

        p.wait("gpsimd", "TQ", p.c("TQ"))
        p.wait("gpsimd", "VS", p.c("VS"))
        src = wo_h.rearrange("(k p) f -> p k f", p=128)
        dma("gpsimd", woB.rearrange("p (k f) -> p k f", k=16), src, "LW")
        p.wait("gpsimd", "SKQ", skq_store)
        dma("gpsimd", kqB, kq_h, "LW")
        lw_b2 = p.c("LW")

        p.wait("tensor", "SEV", p.c("SEV"))
        fst["use"] += 2
        base = fst["use"]
        p.key("mmq%d" % (base - 2), p.c("TQ"))
        p.key("tdma%d" % (base - 1), p.c("TX"))
        for h in range(2):
            p.key("sr%d_%d" % (base - 1, h), p.c("SR"))
        p.key("woev0", 0)
        p.key("woev1", 0)

        def yt_load(j):
            p.wait("gpsimd", "TQ", p.key("womm%d" % (j - 2)))
            ysrc = y_h[:, :, j * 128:(j + 1) * 128].rearrange("c p t -> p c t")
            dma("gpsimd", yTb[j % 2].rearrange("p (c t) -> p c t", c=16),
                ysrc, "LW")
            p.key("ytld%d" % j, p.c("LW"))

        p.wait("gpsimd", "SY", sy_all)
        yt_load(0)

        for j in range(NT):
            jp = j % 2
            p.wait("sync", "SY", sy_all)
            p.wait("sync", "SP", p.key("ffnsq%d" % (fst["use"] - 1)))
            p.wait("sync", "VP", p.key("x2v%d" % (j - 1)))
            dma("sync", dumpV[:, :], x1_h[j * 128:(j + 1) * 128, :], "LD")
            ld_snap = p.c("LD")
            p.wait("tensor", "LD", ld_snap)
            p.wait("tensor", "LW", max(lw_b2, p.key("ytld%d" % j)))
            if stage == "B2":
                p.wait("vector", "SOUT", 16 * (j - 1))
            else:
                p.wait("vector", "VP", p.key("q2f%d" % (fst["use"] - 2)))
            for q4 in range(4):
                bk = bank(4 + (q4 % 2))
                p.wait("tensor", "VP", p.key("woev%d" % (q4 % 2)))
                for kc in range(16):
                    p.op("tensor", lambda e, b=bk, k=kc, q=q4, yb=yTb[jp]:
                         e.matmul(b[:], yb[:, k * 128:(k + 1) * 128],
                                  woB[:, k * 2048 + q * 512: k * 2048 + (q + 1) * 512],
                                  start=(k == 0), stop=False))
                p.op("tensor", lambda e, b=bk, q=q4: e.matmul(
                    b[:], onesb[:], bout_bf[0:1, q * 512:(q + 1) * 512],
                    start=False, stop=True), inc="TQ")
                p.wait("vector", "TQ", p.c("TQ"))
                p.wait("vector", "LD", ld_snap)
                p.op("vector", lambda e, b=bk, q=q4, xbv=xb[jp]: e.tensor_add(
                    xbv[:, q * 512:(q + 1) * 512], b[:],
                    dumpV[:, q * 512:(q + 1) * 512]), inc="VP")
                p.key("woev%d" % (q4 % 2), p.c("VP"))
            p.key("womm%d" % j, p.c("TQ"))
            if j + 1 < NT:
                yt_load(j + 1)
            x2_ready = p.c("VP")
            p.key("x2v%d" % j, x2_ready)
            p.wait("vector", "VP", x2_ready)
            p.wait("scalar", "VP", x2_ready)
            if stage == "B2":
                p.key("ffnxq%d" % fst["use"], x2_ready)
                p.key("ffnsq%d" % fst["use"], 0)
                fst["use"] += 1
                p.wait("sync", "VP", x2_ready)
                p.wait("sync", "SOUT", 16 * (j - 1))
                dma("sync", out_h[j * 128:(j + 1) * 128, :], xb[jp], "SOUT")
                continue
            ffn_front(xb[jp], kqB)
            if j >= 1:
                vpr = ffn_back(fst["use"] - 2, ("SOUT", 16 * (j - 1)))
                p.wait("sync", "VP", vpr)
                dma("sync", out_h[(j - 1) * 128:j * 128, :], ybuf[:], "SOUT")
        if stage != "B2":
            vpr = ffn_back(fst["use"] - 1, ("SOUT", 16 * (NT - 1)))
            p.wait("sync", "VP", vpr)
            dma("sync", out_h[(NT - 1) * 128:NT * 128, :], ybuf[:], "SOUT")
        p.wait("sync", "SOUT", p.c("SOUT"))
        p.wait("gpsimd", "SKQ", skq_store)

        emit_into_block(nc, p, sem_handles)
    return nc


def emit_into_block(nc, p, sems):
    with nc.Block() as block:
        @block.sync
        def _(eng):
            p.replay("sync", eng, sems)

        @block.gpsimd
        def _(eng):
            p.replay("gpsimd", eng, sems)

        @block.tensor
        def _(eng):
            p.replay("tensor", eng, sems)

        @block.vector
        def _(eng):
            p.replay("vector", eng, sems)

        @block.scalar
        def _(eng):
            p.replay("scalar", eng, sems)


def _prep_inputs(inputs):
    def f32(a):
        return np.ascontiguousarray(np.asarray(a, dtype=np.float32))
    x = f32(inputs["L0_attn_x"]).reshape(16, 2048, 2048)
    base = {
        "wi": f32(inputs["L0_attn_W_in"]),
        "wd": f32(inputs["L0_attn_W_decay"]),
        "wg": f32(inputs["L0_attn_W_gate"]),
        "wo": f32(inputs["L0_attn_W_out"]),
        "wk": f32(inputs["L0_ffn_kernel"]),
        "bin": f32(inputs["L0_attn_b_in"]).reshape(2048),
        "bdec": f32(inputs["L0_attn_b_decay"]).reshape(2048),
        "bgate": f32(inputs["L0_attn_b_gate"]).reshape(2048),
        "bout": f32(inputs["L0_attn_b_out"]).reshape(1, 2048),
        "dbias": f32(inputs["L0_attn_decay_bias"]).reshape(2048),
        "fbias": f32(inputs["L0_ffn_bias"]).reshape(1, 2048),
        "eyef": np.eye(128, dtype=np.float32),
        "onesb": np.ones((1, 128), dtype=ml_dtypes.bfloat16),
    }
    in_maps = []
    for c in range(8):
        m = dict(base)
        m["x"] = np.ascontiguousarray(x[2 * c:2 * c + 2].reshape(NTOK, D))
        in_maps.append(m)
    return in_maps


def kernel(**inputs):
    in_maps = _prep_inputs(inputs)
    nc = build_nc("full")
    res = run_bass_kernel_spmd(nc, in_maps, core_ids=list(range(8)))
    outs = [np.asarray(r["out"], dtype=np.float32).reshape(2, 2048, 2048)
            for r in res.results]
    return np.concatenate(outs, axis=0)


# version 19
# speedup vs baseline: 1.5811x; 1.0843x over previous
import contextlib
import os
import numpy as np
import ml_dtypes
import concourse.bass as bass
import concourse.mybir as mybir
from concourse.bass_utils import run_bass_kernel_spmd

F32 = mybir.dt.float32
BF16 = mybir.dt.bfloat16
AX = mybir.AxisListType.X
OP = mybir.AluOpType
AF = mybir.ActivationFunctionType

D = 2048
NTOK = 4096
NT = 32
MAGIC = float(np.float32(1.5 * 2 ** 23))
EPS = 1e-5
INV_D = 1.0 / 2048.0
TINY = 1e-30

ENGS = ("sync", "gpsimd", "tensor", "vector", "scalar")


class Prog:
    def __init__(self):
        self.ops = {e: [] for e in ENGS}
        self.cnt = {}
        self.last_wait = {}
        self.keys = {}

    def sem(self, name):
        self.cnt[name] = 0

    def wait(self, eng, sem, val):
        if val is None or val <= 0:
            return
        key = (eng, sem)
        if self.last_wait.get(key, -1) >= val:
            return
        self.last_wait[key] = val
        self.ops[eng].append(("w", sem, val))

    def op(self, eng, fn, inc=None, n=1, nodrain=False):
        drain = eng in ("scalar", "vector") and not nodrain
        if inc is not None:
            self.cnt[inc] += n
            if drain:
                self.ops[eng].append(("o", fn))
                self.ops[eng].append(("i", lambda e: e.drain(), inc, n))
            else:
                self.ops[eng].append(("i", fn, inc, n))
        else:
            self.ops[eng].append(("o", fn))
            if drain:
                self.ops[eng].append(("o", lambda e: e.drain()))

    def sop(self, fn, inc=None, n=1):
        self.op("scalar", fn, inc=inc, n=n, nodrain=False)

    def c(self, sem):
        return self.cnt[sem]

    def key(self, name, val=None):
        if val is None:
            return self.keys.get(name, 0)
        self.keys[name] = val

    def replay(self, eng_name, eng, sems):
        for item in self.ops[eng_name]:
            if item[0] == "w":
                eng.wait_ge(sems[item[1]], item[2])
            elif item[0] == "i":
                item[1](eng).then_inc(sems[item[2]], item[3])
            else:
                item[1](eng)


def build_nc(stage="full"):
    nc = bass.Bass("TRN2", target_bir_lowering=False)

    x_h = nc.dram_tensor("x", [NTOK, D], F32, kind="ExternalInput").ap()
    wi_h = nc.dram_tensor("wi", [D, D], F32, kind="ExternalInput").ap()
    wd_h = nc.dram_tensor("wd", [D, D], F32, kind="ExternalInput").ap()
    wg_h = nc.dram_tensor("wg", [D, D], F32, kind="ExternalInput").ap()
    wo_h = nc.dram_tensor("wo", [D, D], F32, kind="ExternalInput").ap()
    wk_h = nc.dram_tensor("wk", [D, D], F32, kind="ExternalInput").ap()
    bin_h = nc.dram_tensor("bin", [D], F32, kind="ExternalInput").ap()
    bdec_h = nc.dram_tensor("bdec", [D], F32, kind="ExternalInput").ap()
    bgate_h = nc.dram_tensor("bgate", [D], F32, kind="ExternalInput").ap()
    bout_h = nc.dram_tensor("bout", [1, D], F32, kind="ExternalInput").ap()
    dbias_h = nc.dram_tensor("dbias", [D], F32, kind="ExternalInput").ap()
    fbias_h = nc.dram_tensor("fbias", [1, D], F32, kind="ExternalInput").ap()
    eyef_h = nc.dram_tensor("eyef", [128, 128], F32, kind="ExternalInput").ap()
    ones_h = nc.dram_tensor("onesb", [1, 128], BF16, kind="ExternalInput").ap()

    out_h = nc.dram_tensor("out", [NTOK, D], F32, kind="ExternalOutput").ap()
    x1_h = nc.dram_tensor("x1h", [NTOK, D], F32, kind="Internal").ap()
    y_h = nc.dram_tensor("yh", [16, 128, NTOK], BF16, kind="Internal").ap()
    kq_h = nc.dram_tensor("kqh", [128, 32768], BF16, kind="Internal").ap()

    p = Prog()
    for s in ("LD", "LW", "SX1", "SY", "SOUT", "SKQ", "TP", "VP", "SP",
              "TQ", "SR", "SEV", "VS", "TX"):
        p.sem(s)

    with contextlib.ExitStack() as es:
        ec = es.enter_context
        WB = ec(nc.sbuf_tensor("WB", [128, 75776], BF16))
        FA = ec(nc.sbuf_tensor("FA", [128, 12288], F32))
        statn = ["sums", "sumsq", "sumsq2", "m", "mm", "var", "vpe", "lnv",
                 "rstd", "nrstd", "nmr", "amax", "amcl", "rcp", "inv", "sx",
                 "amax2", "am2", "rcp2", "inv2", "alpha"]
        st2 = [
            {n: ec(nc.sbuf_tensor(f"st{q}_{n}", [128, 1], F32)) for n in statn}
            for q in range(2)
        ]
        stb = {n: ec(nc.sbuf_tensor("stb_" + n, [128, 1], F32))
               for n in ["sums", "sumsq", "sumsq2", "m", "mm", "var", "vpe",
                         "lnv", "nrstd"]}
        rstd_tab = ec(nc.sbuf_tensor("rstd_tab", [128, 32], F32))
        nmr_tab = ec(nc.sbuf_tensor("nmr_tab", [128, 32], F32))
        vpe_tab = ec(nc.sbuf_tensor("vpe_tab", [128, 32], F32))
        m_tab = ec(nc.sbuf_tensor("m_tab", [128, 32], F32))
        lnv_tab = ec(nc.sbuf_tensor("lnv_tab", [128, 32], F32))
        bin_c = ec(nc.sbuf_tensor("bin_c", [128, 16], F32))
        bgate_c = ec(nc.sbuf_tensor("bgate_c", [128, 16], F32))
        bdec_c = ec(nc.sbuf_tensor("bdec_c", [128, 16], F32))
        dbias_c = ec(nc.sbuf_tensor("dbias_c", [128, 16], F32))
        bd_c = ec(nc.sbuf_tensor("bd_c", [128, 16], F32))
        nbd_c = ec(nc.sbuf_tensor("nbd_c", [128, 16], F32))
        fbamax = ec(nc.sbuf_tensor("fbamax", [1, 1], F32))
        fbcl = ec(nc.sbuf_tensor("fbcl", [1, 1], F32))
        fbrcp = ec(nc.sbuf_tensor("fbrcp", [1, 1], F32))
        fbinv = ec(nc.sbuf_tensor("fbinv", [1, 1], F32))
        fbsk = ec(nc.sbuf_tensor("fbsk", [1, 1], F32))
        bq_bf = ec(nc.sbuf_tensor("bq_bf", [1, D], BF16))
        bout_bf = ec(nc.sbuf_tensor("bout_bf", [1, D], BF16))
        carry = ec(nc.sbuf_tensor("carry", [128, 32], F32))
        cmag = ec(nc.sbuf_tensor("cmag", [128, 1], F32))
        cneg = ec(nc.sbuf_tensor("cneg", [128, 1], F32))
        cnh = ec(nc.sbuf_tensor("cnh", [128, 1], F32))
        eyef = ec(nc.sbuf_tensor("eyef_s", [128, 128], F32))
        onesb = ec(nc.sbuf_tensor("onesb_s", [1, 128], BF16))
        pz = ec(nc.psum_tensor("pz", [128, 4096], F32))
        sem_handles = {nm: ec(nc.semaphore(nm)) for nm in p.cnt.keys()}

        def bank(i):
            return pz[:, i * 512:(i + 1) * 512]

        def bank2(i):
            return pz[:, i * 512:(i + 2) * 512]

        def wb(a, b):
            return WB[:, a:b]

        def fa(a, b):
            return FA[:, a:b]

        xb = [fa(0, 2048), fa(2048, 4096)]
        dumpV = fa(4096, 6144)
        dumpS = [fa(6144, 8192), fa(8192, 10240)]
        ybuf = fa(10240, 12288)
        d_stg = [fa(6144, 7168), fa(7168, 8192)]
        b_stg = [fa(8192, 9216), fa(9216, 10240)]
        s_stg = [fa(10240, 11264), fa(11264, 12288)]

        kqA = wb(0, 32768)
        yTb = [wb(65536, 67584), wb(67584, 69632)]
        qb_s = wb(69632, 71680)
        qbuf = [qb_s, qb_s]
        qT = [wb(71680, 73728), wb(73728, 75776)]
        yb16 = [qb_s, qb_s]
        kqTst = wb(45056, 47104)
        ln2T = [wb(0, 16384), wb(16384, 32768)]
        wslot = [wb(32768 + i * 2048, 32768 + (i + 1) * 2048) for i in range(6)]
        ln2b = [wb(45056, 47104), wb(47104, 49152)]
        tsig_st = wb(49152, 50176)
        z_st = wb(50176, 51200)
        inp_st = [wb(51200, 52224), wb(52224, 53248)]
        e_stg = [wb(53248, 54272), wb(54272, 55296)]
        gate_st = [wb(55296, 56320), wb(56320, 57344)]
        y_stg = [wb(57344, 58368), wb(58368, 59392)]
        woB = wb(0, 32768)
        kqB = wb(32768, 65536)

        def dma(eng, out, in_, inc, slow=False):
            if slow:
                p.op(eng, lambda e, o=out, i=in_: e.dma_start(
                    out=o, in_=i, allow_slow_non_contiguous=True), inc=inc, n=16)
            else:
                p.op(eng, lambda e, o=out, i=in_: e.dma_start(out=o, in_=i),
                     inc=inc, n=16)

        def tdma(out, in_):
            p.op("sync", lambda e, o=out, i=in_: e.dma_start(out=o, in_=i,
                                                             transpose=True),
                 inc="TX", n=16)

        p.op("vector", lambda e: e.memset(cmag[:], MAGIC))
        p.op("vector", lambda e: e.memset(cneg[:], -1.0))
        p.op("vector", lambda e: e.memset(cnh[:], -0.5))
        nc.const_aps.register(F32, MAGIC, cmag[:])
        nc.const_aps.register(F32, -1.0, cneg[:])
        nc.const_aps.register(F32, -0.5, cnh[:])
        dma("sync", eyef[:], eyef_h, "LD")
        dma("sync", onesb[:], ones_h, "LD")
        fbrow = FA[0:1, 4096:6144]
        fbt = FA[0:1, 6144:8192]
        dma("sync", fbrow, fbias_h, "LD")
        dma("gpsimd", bin_c[:], bin_h.rearrange("(c p) -> p c", p=128), "LW", slow=True)
        dma("gpsimd", bgate_c[:], bgate_h.rearrange("(c p) -> p c", p=128), "LW", slow=True)
        dma("gpsimd", bdec_c[:], bdec_h.rearrange("(c p) -> p c", p=128), "LW", slow=True)
        dma("gpsimd", dbias_c[:], dbias_h.rearrange("(c p) -> p c", p=128), "LW", slow=True)
        dma("gpsimd", bout_bf[:], bout_h, "LW")
        ld_consts = p.c("LD")
        lw_consts = p.c("LW")

        p.wait("vector", "LW", lw_consts)
        p.op("vector", lambda e: e.tensor_add(bd_c[:], bdec_c[:], dbias_c[:]))
        p.op("vector", lambda e: e.tensor_scalar(nbd_c[:], bd_c[:], -1.0, None,
                                                 op0=OP.mult))
        p.op("vector", lambda e: e.memset(carry[:], 0.0))
        p.wait("vector", "LD", ld_consts)
        p.op("vector", lambda e: e.reduce_max(fbamax[:], fbrow, axis=AX,
                                              apply_absolute_value=True))
        p.op("vector", lambda e: e.tensor_scalar(fbcl[:], fbamax[:], TINY, None,
                                                 op0=OP.max))
        p.op("vector", lambda e: e.reciprocal(fbrcp[:], fbcl[:]))
        p.op("vector", lambda e: e.tensor_scalar(fbinv[:], fbrcp[:], 127.0, None,
                                                 op0=OP.mult))
        p.op("vector", lambda e: e.tensor_scalar(fbsk[:], fbcl[:], 1.0 / 127.0,
                                                 None, op0=OP.mult))
        p.op("vector", lambda e: e.tensor_scalar(fbt, fbrow, fbinv[0:1, :], MAGIC,
                                                 op0=OP.mult, op1=OP.add))
        p.op("vector", lambda e: e.tensor_scalar(bq_bf[:], fbt, -MAGIC,
                                                 fbsk[0:1, :],
                                                 op0=OP.add, op1=OP.mult), inc="VP")

        st = st2[0]
        wkb = [qT[0], qT[1]]
        wkT = qbuf[0]
        for fc in range(16):
            buf = wkb[fc % 2]
            p.wait("gpsimd", "TX", p.key("kqld%d" % (fc - 2)))
            src_ap = wk_h[:, fc * 128:(fc + 1) * 128].rearrange("(k p) f -> p k f", p=128)
            dma("gpsimd", buf.rearrange("p (k f) -> p k f", k=16), src_ap, "LW")
            ld_snap = p.c("LW")
            p.wait("sync", "LW", ld_snap)
            p.wait("sync", "VP", p.key("kqq%d" % (fc - 1)))
            tdma(wkT.rearrange("p (c t) -> p c t", c=16), buf)
            p.key("kqld%d" % fc, p.c("TX"))
            p.wait("vector", "TX", p.c("TX"))
            p.op("vector", lambda e: e.reduce_max(st["amax"][:], wkT[:], axis=AX,
                                                  apply_absolute_value=True))
            p.op("vector", lambda e: e.tensor_scalar(st["amcl"][:], st["amax"][:],
                                                     TINY, None, op0=OP.max))
            p.op("vector", lambda e: e.reciprocal(st["rcp"][:], st["amcl"][:]))
            p.op("vector", lambda e: e.tensor_scalar(st["inv"][:], st["rcp"][:],
                                                     127.0, None, op0=OP.mult))
            p.op("vector", lambda e: e.tensor_scalar(st["sx"][:], st["amcl"][:],
                                                     1.0 / 127.0, None, op0=OP.mult))
            p.op("vector", lambda e: e.tensor_scalar(dumpS[0], wkT[:], st["inv"][:],
                                                     MAGIC, op0=OP.mult, op1=OP.add))
            p.op("vector", lambda e: e.tensor_scalar(kqTst[:], dumpS[0], -MAGIC,
                                                     st["sx"][:],
                                                     op0=OP.add, op1=OP.mult), inc="VP")
            p.key("kqq%d" % fc, p.c("VP"))
            p.wait("sync", "VP", p.c("VP"))
            kq3 = kqA.rearrange("p (k g) -> p k g", k=16)
            tdma(kq3[:, :, fc * 128:(fc + 1) * 128], kqTst[:])
            p.wait("vector", "TX", p.c("TX"))
        tx_kq_done = p.c("TX")
        prep_vp = p.c("VP")
        p.wait("gpsimd", "TX", tx_kq_done)
        dma("gpsimd", kq_h, kqA, "SKQ")
        skq_store = p.c("SKQ")

        fst = {"use": 0, "info": {}}

        def ffn_front(xsrc, kq_region, tab_j=None):
            u = fst["use"]
            fst["use"] = u + 1
            pr = u % 2
            st = st2[pr]
            fst["info"][u] = (xsrc, kq_region, tab_j)
            p.op("vector", lambda e: e.reduce_sum(st["sums"][:], xsrc, axis=AX))
            p.op("scalar", lambda e: e.activation(
                bank2(6), xsrc[:, 0:1024], AF.Square,
                accum_out=st["sumsq"][:]))
            p.op("scalar", lambda e: e.activation(
                bank2(6), xsrc[:, 1024:2048], AF.Square,
                accum_out=st["sumsq2"][:]), inc="SP")
            p.key("ffnsq%d" % u, p.c("SP"))
            p.wait("vector", "SP", p.c("SP"))
            p.op("vector", lambda e: e.tensor_add(st["sumsq"][:], st["sumsq"][:],
                                                  st["sumsq2"][:]))
            p.op("vector", lambda e: e.tensor_scalar(st["m"][:], st["sums"][:],
                                                     INV_D, None, op0=OP.mult))
            p.op("vector", lambda e: e.tensor_mul(st["mm"][:], st["m"][:], st["m"][:]))
            p.op("vector", lambda e: e.scalar_tensor_tensor(
                st["var"][:], st["sumsq"][:], INV_D, st["mm"][:],
                op0=OP.mult, op1=OP.subtract))
            p.op("vector", lambda e: e.tensor_scalar(st["vpe"][:], st["var"][:],
                                                     EPS, None, op0=OP.add), inc="VP")
            p.wait("scalar", "VP", p.c("VP"))
            p.sop(lambda e: e.activation(st["lnv"][:], st["vpe"][:], AF.Ln))
            p.sop(lambda e: e.activation(st["rstd"][:], st["lnv"][:],
                                         AF.Exp, scale=-0.5))
            p.sop(lambda e: e.activation(st["nrstd"][:], st["rstd"][:],
                                         AF.Identity, scale=-1.0))
            p.sop(lambda e: e.activation(st["nmr"][:], st["m"][:],
                                         AF.Identity, scale=st["nrstd"][:]))
            p.op("scalar", lambda e, pp=pr: e.activation(dumpS[pp], xsrc, AF.Identity,
                                                         bias=st["nmr"][:],
                                                         scale=st["rstd"][:]), inc="SP")
            p.wait("vector", "SP", p.c("SP"))
            p.op("vector", lambda e, pp=pr: e.reduce_max(st["amax"][:], dumpS[pp],
                                                         axis=AX,
                                                         apply_absolute_value=True))
            p.op("vector", lambda e: e.tensor_scalar(st["amcl"][:], st["amax"][:],
                                                     TINY, None, op0=OP.max))
            p.op("vector", lambda e: e.reciprocal(st["rcp"][:], st["amcl"][:]))
            p.op("vector", lambda e: e.tensor_scalar(st["inv"][:], st["rcp"][:],
                                                     127.0, None, op0=OP.mult))
            p.op("vector", lambda e: e.tensor_scalar(st["sx"][:], st["amcl"][:],
                                                     1.0 / 127.0, None, op0=OP.mult))
            p.op("vector", lambda e, pp=pr: e.tensor_scalar(
                dumpS[1 - pp], dumpS[pp], st["inv"][:], MAGIC,
                op0=OP.mult, op1=OP.add))
            p.wait("vector", "TX", p.key("tdma%d" % (u - 1)))
            p.op("vector", lambda e, pp=pr, q=qbuf[pr]: e.tensor_scalar(
                q, dumpS[1 - pp], -MAGIC, st["sx"][:], op0=OP.add, op1=OP.mult),
                inc="VP")
            vp_xq = p.c("VP")
            p.key("ffnxq%d" % u, vp_xq)
            p.wait("sync", "VP", vp_xq)
            p.wait("sync", "TQ", p.key("mmq%d" % (u - 2)))
            p.wait("sync", "VP", p.key("q2f%d" % (u - 2)))
            tdma(qT[pr].rearrange("p (c t) -> p c t", c=16), qbuf[pr])
            p.key("tdma%d" % u, p.c("TX"))

        def ffn_back(u, yb_gate):
            pr = u % 2
            st = st2[pr]
            xsrc, kq_region, tab_j = fst["info"].pop(u)
            p.wait("tensor", "TX", p.key("tdma%d" % u))
            for q4 in range(4):
                bk = bank(q4)
                p.wait("tensor", "SR", p.key("sr%d_%d" % (u - 1, q4 // 2)))
                for kc in range(16):
                    p.op("tensor", lambda e, b=bk, k=kc, jp=pr, q=q4, kr=kq_region:
                         e.matmul(b[:], qT[jp][:, k * 128:(k + 1) * 128],
                                  kr[:, k * 2048 + q * 512: k * 2048 + (q + 1) * 512],
                                  start=(k == 0), stop=False))
                p.op("tensor", lambda e, b=bk, q=q4: e.matmul(
                    b[:], onesb[:], bq_bf[0:1, q * 512:(q + 1) * 512],
                    start=False, stop=True), inc="TQ")
                if q4 % 2 == 1:
                    tq_snap = p.c("TQ")
                    p.wait("scalar", "TQ", tq_snap)
                    if yb_gate is not None:
                        p.wait("scalar", yb_gate[0], yb_gate[1])
                    h = q4 // 2
                    p.op("scalar", lambda e, hh=h: e.activation(
                        ybuf[:, hh * 1024:(hh + 1) * 1024], bank2(hh * 2), AF.Relu),
                        inc="SR")
                    p.key("sr%d_%d" % (u, h), p.c("SR"))
            p.key("mmq%d" % u, p.c("TQ"))
            p.wait("vector", "SR", p.c("SR"))
            p.op("vector", lambda e: e.reduce_max(st["amax2"][:], ybuf[:], axis=AX))
            p.op("vector", lambda e: e.tensor_scalar(st["am2"][:], st["amax2"][:],
                                                     TINY, None, op0=OP.max))
            p.op("vector", lambda e: e.reciprocal(st["rcp2"][:], st["am2"][:]))
            p.op("vector", lambda e: e.tensor_scalar(st["inv2"][:], st["rcp2"][:],
                                                     127.0, None, op0=OP.mult))
            p.op("vector", lambda e: e.tensor_scalar(st["alpha"][:], st["am2"][:],
                                                     0.5 / 127.0, None, op0=OP.mult))
            p.op("vector", lambda e, pp=pr: e.tensor_scalar(
                dumpS[pp], ybuf[:], st["inv2"][:], MAGIC, op0=OP.mult, op1=OP.add))
            p.wait("vector", "TQ", p.key("mmq%d" % u))
            p.op("vector", lambda e, q=qT[pr], pp=pr: e.tensor_scalar(
                q, dumpS[pp], -MAGIC, None, op0=OP.add), inc="VP")
            p.key("q2bv%d" % u, p.c("VP"))
            if yb_gate is not None:
                p.wait("vector", yb_gate[0], yb_gate[1])
            p.wait("vector", "SP", p.key("tabsq%d" % (u - 1)))
            p.op("vector", lambda e, q=qT[pr]: e.scalar_tensor_tensor(
                ybuf[:], q, st["alpha"][:], xsrc, op0=OP.mult, op1=OP.add),
                inc="VP")
            res_vp = p.c("VP")
            p.key("q2f%d" % u, res_vp)
            if tab_j is not None:
                p.op("vector", lambda e: e.reduce_sum(stb["sums"][:], ybuf[:],
                                                      axis=AX))
                p.wait("scalar", "VP", res_vp)
                p.op("scalar", lambda e: e.activation(
                    bank2(6), ybuf[:, 0:1024], AF.Square,
                    accum_out=stb["sumsq"][:]))
                p.op("scalar", lambda e: e.activation(
                    bank2(6), ybuf[:, 1024:2048], AF.Square,
                    accum_out=stb["sumsq2"][:]), inc="SP")
                p.key("tabsq%d" % u, p.c("SP"))
                p.wait("vector", "SP", p.c("SP"))
                p.op("vector", lambda e: e.tensor_add(stb["sumsq"][:],
                                                      stb["sumsq"][:],
                                                      stb["sumsq2"][:]))
                p.op("vector", lambda e, t=tab_j: e.tensor_scalar(
                    m_tab[:, t:t + 1], stb["sums"][:], INV_D, None, op0=OP.mult))
                p.op("vector", lambda e, t=tab_j: e.tensor_mul(
                    stb["mm"][:], m_tab[:, t:t + 1], m_tab[:, t:t + 1]))
                p.op("vector", lambda e, t=tab_j: e.scalar_tensor_tensor(
                    stb["var"][:], stb["sumsq"][:], INV_D, stb["mm"][:],
                    op0=OP.mult, op1=OP.subtract))
                p.op("vector", lambda e, t=tab_j: e.tensor_scalar(
                    vpe_tab[:, t:t + 1], stb["var"][:], EPS, None, op0=OP.add),
                    inc="VP")
            return res_vp

        p.wait("tensor", "VP", prep_vp)
        xb3 = [xb[0], xb[1], dumpV]
        xld = {}

        def x_load(j):
            p.wait("sync", "SX1", 16 * (j - 2))
            dma("sync", xb3[j % 3], x_h[j * 128:(j + 1) * 128, :], "LD")
            xld[j] = p.c("LD")

        p.wait("sync", "TX", tx_kq_done)
        for j in range(3):
            x_load(j)
        for j in range(NT):
            p.wait("vector", "LD", xld[j])
            p.wait("scalar", "LD", xld[j])
            ffn_front(xb3[j % 3], kqA, tab_j=j)
            if j >= 1:
                vpr = ffn_back(fst["use"] - 2, ("SX1", 16 * (j - 1)))
                p.wait("sync", "VP", vpr)
                dma("sync", x1_h[(j - 1) * 128:j * 128, :], ybuf[:], "SX1")
                if j + 2 < NT:
                    x_load(j + 2)
        vpr = ffn_back(fst["use"] - 1, ("SX1", 16 * (NT - 1)))
        p.wait("sync", "VP", vpr)
        dma("sync", x1_h[(NT - 1) * 128:NT * 128, :], ybuf[:], "SX1")
        sx1_all = p.c("SX1")
        p.wait("scalar", "VP", p.c("VP"))
        p.sop(lambda e: e.activation(lnv_tab[:], vpe_tab[:], AF.Ln))
        p.op("scalar", lambda e: e.activation(rstd_tab[:], lnv_tab[:], AF.Exp,
                                              scale=-0.5), inc="SP")
        p.wait("vector", "SP", p.c("SP"))
        p.op("vector", lambda e: e.tensor_mul(lnv_tab[:], m_tab[:], rstd_tab[:]))
        p.op("vector", lambda e: e.tensor_scalar(nmr_tab[:], lnv_tab[:], -1.0,
                                                 None, op0=OP.mult), inc="VP")
        tabs_vp = p.c("VP")

        if stage == "A":
            o_tab = nc.dram_tensor("tabdump", [128, 64], F32,
                                   kind="ExternalOutput").ap()
            p.wait("sync", "SX1", sx1_all)
            p.wait("sync", "SP", p.c("SP"))
            dma("sync", out_h, x1_h, "SOUT")
            dma("sync", o_tab[:, 0:32], rstd_tab[:], "SOUT")
            dma("sync", o_tab[:, 32:64], nmr_tab[:], "SOUT")
            p.wait("sync", "SOUT", p.c("SOUT"))
            p.wait("gpsimd", "SKQ", skq_store)
            emit_into_block(nc, p, sem_handles)
            return nc

        wsrcs = [wi_h, wg_h, wd_h]

        def emit_wload(c):
            base = (c % 2) * 3
            for wi_, wsrc in enumerate(wsrcs):
                src = wsrc[:, c * 128:(c + 1) * 128].rearrange("(k p) f -> p k f", p=128)
                dst = wslot[base + wi_].rearrange("p (k f) -> p k f", k=16)
                dma("gpsimd", dst, src, "LW")
            p.key("wl%d" % (c % 2), p.c("LW"))

        emit_wload(0)
        emit_wload(1)

        p.wait("tensor", "SR", p.c("SR"))
        b1 = {"pmuse": 0}
        lnsnaps = []

        for s in range(4):
            tiles = [4 * s + i for i in range(4)] + [16 + 4 * s + i for i in range(4)]
            for i, tj in enumerate(tiles):
                gi = s * 8 + i
                p.wait("sync", "SX1", sx1_all)
                if gi >= 2:
                    p.wait("sync", "SP", lnsnaps[gi - 2])
                dma("sync", xb[i % 2], x1_h[tj * 128:(tj + 1) * 128, :], "LD")
                ld_snap = p.c("LD")
                p.wait("scalar", "LD", ld_snap)
                p.wait("scalar", "VP", tabs_vp)
                p.wait("scalar", "TX", p.key("b1tt%d" % (gi - 2)))
                p.op("scalar", lambda e, b=xb[i % 2], o=ln2b[i % 2], t=tj:
                     e.activation(o[:], b, AF.Identity,
                                  bias=nmr_tab[:, t:t + 1],
                                  scale=rstd_tab[:, t:t + 1]), inc="SP")
                lnsnaps.append(p.c("SP"))
                p.wait("sync", "SP", p.c("SP"))
                p.wait("sync", "TQ", p.key("b1mm%d" % (s - 2)))
                l3 = ln2T[s % 2].rearrange("p (k t) -> p k t", k=16)
                tdma(l3[:, :, i * 128:(i + 1) * 128], ln2b[i % 2][:])
                p.key("b1tt%d" % gi, p.c("TX"))
            tx_span_ready = p.c("TX")

            for c in range(16):
                gc = s * 16 + c
                cs = c % 2
                p.wait("tensor", "LW", p.key("wl%d" % (c % 2)))
                p.wait("tensor", "TX", tx_span_ready)
                banks = {}
                for wi_ in range(3):
                    for h in range(2):
                        u = b1["pmuse"]
                        bk = bank(u % 8)
                        p.wait("tensor", "SEV", p.key("pev%d" % (u % 8)))
                        b1["pmuse"] = u + 1
                        banks[(wi_, h)] = (bk, u)
                        for kc in range(16):
                            p.op("tensor",
                                 lambda e, b=bk, k=kc, w=wslot[(c % 2) * 3 + wi_],
                                 sp_=s, hh=h:
                                 e.matmul(b[:], w[:, k * 128:(k + 1) * 128],
                                          ln2T[sp_ % 2][:, k * 1024 + hh * 512:
                                                        k * 1024 + (hh + 1) * 512],
                                          start=(k == 0), stop=(k == 15)),
                                 inc=("TQ" if kc == 15 else None))
                        p.key("b1u%d" % u, p.c("TQ"))
                p.key("b1mm%d" % s, p.c("TQ"))
                tq_chunk = p.c("TQ")
                nxt = c + 2
                if nxt < 16:
                    p.wait("gpsimd", "TQ", tq_chunk)
                    emit_wload(nxt)
                elif s < 3:
                    p.wait("gpsimd", "TQ", tq_chunk)
                    emit_wload(nxt - 16)
                p.wait("scalar", "VS", p.key("b1v%d" % (gc - 1)))
                for h in range(2):
                    bk, u = banks[(0, h)]
                    p.wait("scalar", "TQ", p.key("b1u%d" % u))
                    p.op("scalar", lambda e, bb=bk, hh=h, bc=bin_c[:, c:c + 1]:
                         e.activation(tsig_st[:, hh * 512:(hh + 1) * 512], bb[:],
                                      AF.Sigmoid, bias=bc), inc="SP")
                    p.op("scalar", lambda e, bb=bk, hh=h, bc=bin_c[:, c:c + 1]:
                         e.activation(z_st[:, hh * 512:(hh + 1) * 512], bb[:],
                                      AF.Identity, bias=bc), inc="SEV")
                    p.key("pev%d" % (u % 8), p.c("SEV"))
                for h in range(2):
                    bk, u = banks[(1, h)]
                    p.wait("scalar", "TQ", p.key("b1u%d" % u))
                    p.op("scalar", lambda e, bb=bk, hh=h, bc=bgate_c[:, c:c + 1],
                         g=gate_st[cs]:
                         e.activation(g[:, hh * 512:(hh + 1) * 512], bb[:],
                                      AF.Sigmoid, bias=bc), inc="SEV")
                    p.key("pev%d" % (u % 8), p.c("SEV"))
                for h in range(2):
                    bk, u = banks[(2, h)]
                    p.wait("scalar", "TQ", p.key("b1u%d" % u))
                    p.op("scalar", lambda e, bb=bk, hh=h, bc=nbd_c[:, c:c + 1],
                         ee=e_stg[cs]:
                         e.activation(ee[:, hh * 512:(hh + 1) * 512], bb[:],
                                      AF.Sigmoid, bias=bc, scale=cneg[:]), inc="SP")
                    p.op("scalar", lambda e, bb=bk, hh=h, bc=bd_c[:, c:c + 1],
                         dd=d_stg[cs]:
                         e.activation(dd[:, hh * 512:(hh + 1) * 512], bb[:],
                                      AF.Sigmoid, bias=bc), inc="SEV")
                    p.key("pev%d" % (u % 8), p.c("SEV"))
                sev_chunk = p.c("SEV")
                sp_chunk = p.c("SP")
                p.wait("vector", "SEV", sev_chunk)
                p.wait("vector", "SP", sp_chunk)
                p.op("vector", lambda e, ii=inp_st[cs]: e.tensor_mul(ii[:], tsig_st[:],
                                                                     z_st[:]))
                p.op("vector", lambda e, ii=inp_st[cs], ee=e_stg[cs], bb=b_stg[cs]:
                     e.tensor_mul(bb[:], ee[:], ii[:]))
                for h in range(2):
                    p.op("vector", lambda e, dd=d_stg[cs], bb=b_stg[cs], ss=s_stg[cs],
                         hh=h, cc=c:
                         e.tensor_tensor_scan(ss[:, hh * 512:(hh + 1) * 512],
                                              dd[:, hh * 512:(hh + 1) * 512],
                                              bb[:, hh * 512:(hh + 1) * 512],
                                              carry[:, cc * 2 + hh:cc * 2 + hh + 1],
                                              op0=OP.mult, op1=OP.add))
                for h in range(2):
                    p.op("vector", lambda e, ss=s_stg[cs], hh=h, cc=c: e.tensor_copy(
                        carry[:, cc * 2 + hh:cc * 2 + hh + 1],
                        ss[:, hh * 512 + 511:hh * 512 + 512]))
                p.wait("vector", "SY", p.key("b1y%d" % (gc - 2)))
                p.op("vector", lambda e, g=gate_st[cs], ss=s_stg[cs], yy=y_stg[cs]:
                     e.tensor_mul(yy[:], g[:], ss[:]), inc="VS")
                p.key("b1v%d" % gc, p.c("VS"))
                p.wait("sync", "VS", p.c("VS"))
                ydst = y_h[c, :, :].rearrange("p (b t) -> p b t", b=2)[:, :,
                                                                      s * 512:(s + 1) * 512]
                dma("sync", ydst, y_stg[cs][:].rearrange("p (b t) -> p b t", b=2),
                    "SY")
                p.key("b1y%d" % gc, p.c("SY"))
        sy_all = p.c("SY")

        if stage == "B1":
            o_yd = nc.dram_tensor("ydump", [16, 128, NTOK], BF16,
                                  kind="ExternalOutput").ap()
            p.wait("gpsimd", "SY", sy_all)
            for cc in range(16):
                p.op("gpsimd", lambda e, c=cc: e.dma_start(
                    out=o_yd[c], in_=y_h[c]), inc="SOUT", n=16)
            p.wait("gpsimd", "SOUT", p.c("SOUT"))
            p.wait("gpsimd", "SKQ", skq_store)
            emit_into_block(nc, p, sem_handles)
            return nc

        p.wait("gpsimd", "TQ", p.c("TQ"))
        p.wait("gpsimd", "VS", p.c("VS"))
        src = wo_h.rearrange("(k p) f -> p k f", p=128)
        dma("gpsimd", woB.rearrange("p (k f) -> p k f", k=16), src, "LW")
        p.wait("gpsimd", "SKQ", skq_store)
        dma("gpsimd", kqB, kq_h, "LW")
        lw_b2 = p.c("LW")

        p.wait("tensor", "SEV", p.c("SEV"))
        fst["use"] += 2
        base = fst["use"]
        p.key("mmq%d" % (base - 2), p.c("TQ"))
        p.key("tdma%d" % (base - 1), p.c("TX"))
        for h in range(2):
            p.key("sr%d_%d" % (base - 1, h), p.c("SR"))
        p.key("woev0", 0)
        p.key("woev1", 0)

        def yt_load(j):
            p.wait("gpsimd", "TQ", p.key("womm%d" % (j - 2)))
            ysrc = y_h[:, :, j * 128:(j + 1) * 128].rearrange("c p t -> p c t")
            dma("gpsimd", yTb[j % 2].rearrange("p (c t) -> p c t", c=16),
                ysrc, "LW")
            p.key("ytld%d" % j, p.c("LW"))

        p.wait("gpsimd", "SY", sy_all)
        yt_load(0)
        ldv = {}
        p.wait("sync", "SY", sy_all)
        p.wait("sync", "SP", p.key("ffnsq%d" % (fst["use"] - 1)))
        dma("sync", dumpV[:, :], x1_h[0:128, :], "LD")
        ldv[0] = p.c("LD")

        for j in range(NT):
            jp = j % 2
            ld_snap = ldv[j]
            p.wait("tensor", "LD", ld_snap)
            p.wait("tensor", "LW", max(lw_b2, p.key("ytld%d" % j)))
            if stage == "B2":
                p.wait("vector", "SOUT", 16 * (j - 1))
            else:
                p.wait("vector", "VP", p.key("q2f%d" % (fst["use"] - 2)))
            for q4 in range(4):
                bk = bank(4 + (q4 % 2))
                p.wait("tensor", "VP", p.key("woev%d" % (q4 % 2)))
                for kc in range(16):
                    p.op("tensor", lambda e, b=bk, k=kc, q=q4, yb=yTb[jp]:
                         e.matmul(b[:], yb[:, k * 128:(k + 1) * 128],
                                  woB[:, k * 2048 + q * 512: k * 2048 + (q + 1) * 512],
                                  start=(k == 0), stop=False))
                p.op("tensor", lambda e, b=bk, q=q4: e.matmul(
                    b[:], onesb[:], bout_bf[0:1, q * 512:(q + 1) * 512],
                    start=False, stop=True), inc="TQ")
                p.wait("vector", "TQ", p.c("TQ"))
                p.wait("vector", "LD", ld_snap)
                p.op("vector", lambda e, b=bk, q=q4, xbv=xb[jp]: e.tensor_add(
                    xbv[:, q * 512:(q + 1) * 512], b[:],
                    dumpV[:, q * 512:(q + 1) * 512]), inc="VP")
                p.key("woev%d" % (q4 % 2), p.c("VP"))
            p.key("womm%d" % j, p.c("TQ"))
            if j + 1 < NT:
                yt_load(j + 1)
            x2_ready = p.c("VP")
            p.key("x2v%d" % j, x2_ready)
            p.wait("vector", "VP", x2_ready)
            p.wait("scalar", "VP", x2_ready)
            if stage == "B2":
                p.key("ffnxq%d" % fst["use"], x2_ready)
                p.key("ffnsq%d" % fst["use"], 0)
                fst["use"] += 1
                p.wait("sync", "VP", x2_ready)
                p.wait("sync", "SOUT", 16 * (j - 1))
                dma("sync", out_h[j * 128:(j + 1) * 128, :], xb[jp], "SOUT")
                if j + 1 < NT:
                    dma("sync", dumpV[:, :],
                        x1_h[(j + 1) * 128:(j + 2) * 128, :], "LD")
                    ldv[j + 1] = p.c("LD")
                continue
            ffn_front(xb[jp], kqB)
            if j + 1 < NT:
                dma("sync", dumpV[:, :],
                    x1_h[(j + 1) * 128:(j + 2) * 128, :], "LD")
                ldv[j + 1] = p.c("LD")
            if j >= 1:
                vpr = ffn_back(fst["use"] - 2, ("SOUT", 16 * (j - 1)))
                p.wait("sync", "VP", vpr)
                dma("sync", out_h[(j - 1) * 128:j * 128, :], ybuf[:], "SOUT")
        if stage != "B2":
            vpr = ffn_back(fst["use"] - 1, ("SOUT", 16 * (NT - 1)))
            p.wait("sync", "VP", vpr)
            dma("sync", out_h[(NT - 1) * 128:NT * 128, :], ybuf[:], "SOUT")
        p.wait("sync", "SOUT", p.c("SOUT"))
        p.wait("gpsimd", "SKQ", skq_store)

        emit_into_block(nc, p, sem_handles)
    return nc


def emit_into_block(nc, p, sems):
    with nc.Block() as block:
        @block.sync
        def _(eng):
            p.replay("sync", eng, sems)

        @block.gpsimd
        def _(eng):
            p.replay("gpsimd", eng, sems)

        @block.tensor
        def _(eng):
            p.replay("tensor", eng, sems)

        @block.vector
        def _(eng):
            p.replay("vector", eng, sems)

        @block.scalar
        def _(eng):
            p.replay("scalar", eng, sems)


def _prep_inputs(inputs):
    def f32(a):
        return np.ascontiguousarray(np.asarray(a, dtype=np.float32))
    x = f32(inputs["L0_attn_x"]).reshape(16, 2048, 2048)
    base = {
        "wi": f32(inputs["L0_attn_W_in"]),
        "wd": f32(inputs["L0_attn_W_decay"]),
        "wg": f32(inputs["L0_attn_W_gate"]),
        "wo": f32(inputs["L0_attn_W_out"]),
        "wk": f32(inputs["L0_ffn_kernel"]),
        "bin": f32(inputs["L0_attn_b_in"]).reshape(2048),
        "bdec": f32(inputs["L0_attn_b_decay"]).reshape(2048),
        "bgate": f32(inputs["L0_attn_b_gate"]).reshape(2048),
        "bout": f32(inputs["L0_attn_b_out"]).reshape(1, 2048),
        "dbias": f32(inputs["L0_attn_decay_bias"]).reshape(2048),
        "fbias": f32(inputs["L0_ffn_bias"]).reshape(1, 2048),
        "eyef": np.eye(128, dtype=np.float32),
        "onesb": np.ones((1, 128), dtype=ml_dtypes.bfloat16),
    }
    in_maps = []
    for c in range(8):
        m = dict(base)
        m["x"] = np.ascontiguousarray(x[2 * c:2 * c + 2].reshape(NTOK, D))
        in_maps.append(m)
    return in_maps


def kernel(**inputs):
    in_maps = _prep_inputs(inputs)
    nc = build_nc("full")
    res = run_bass_kernel_spmd(nc, in_maps, core_ids=list(range(8)))
    outs = [np.asarray(r["out"], dtype=np.float32).reshape(2, 2048, 2048)
            for r in res.results]
    return np.concatenate(outs, axis=0)
